# version 02 (final)
# baseline (speedup 1.0000x reference)
import numpy as np
import ml_dtypes


import concourse.bass as bass
import concourse.mybir as mybir
from concourse.bass_utils import run_bass_kernel_spmd

F32 = mybir.dt.float32
BF16 = mybir.dt.bfloat16
I32 = mybir.dt.int32
AF = mybir.ActivationFunctionType
ALU = mybir.AluOpType
AX = mybir.AxisListType

NCORE = 8
D = 1024
B = 2
S = 8192
TOK = 2048


class Sem:
    __slots__ = ("h", "n")

    def __init__(self, h):
        self.h = h
        self.n = 0

    def inc(self, instr, k=1):
        instr.then_inc(self.h, k)
        self.n += k
        return self.n


class SemPair:
    def __init__(self, ctx, name):
        self.s = [ctx.sem(name + "_e"), ctx.sem(name + "_o")]

    def slot(self, i):
        return self.s[i % 2]


class Ctx:
    def __init__(self, nc, block):
        self.nc = nc
        self.block = block
        self.sems = {}
        self._sb = {}

    def sem(self, name):
        if name not in self.sems:
            h = self.nc.alloc_semaphore(name=name)
            self.sems[name] = Sem(h)
        return self.sems[name]

    def sb(self, name, shape, dtype):
        if name not in self._sb:
            self._sb[name] = self.nc.alloc_sbuf_tensor(name, shape, dtype)
        return self._sb[name]

    def v(self, fn):
        self.block.vector(fn)

    def t(self, fn):
        self.block.tensor(fn)

    def s(self, fn):
        self.block.scalar(fn)

    def g(self, fn):
        self.block.gpsimd(fn)

    def y(self, fn):
        self.block.sync(fn)


def ew(ctx, eng, instr, sname="dv"):
    s = ctx.sem(sname)
    s.inc(instr)
    eng.wait_ge(s.h, s.n)
    return instr


def finish(ctx, done_sem_list):
    def gf(g):
        for sem, val in done_sem_list:
            g.wait_ge(sem.h, val)
    ctx.g(gf)


def emit_ln(ctx, tag, x_dram, xn_dram, T, out_pair):
    nt = T // 128
    xs = ctx.sb(f"{tag}_x", [128, 2 * D], F32)
    sq = ctx.sb(f"{tag}_sq", [128, D], F32)
    st = ctx.sb(f"{tag}_st", [128, 8], F32)
    xo = ctx.sb(f"{tag}_xo", [128, 2 * D], BF16)
    ld = SemPair(ctx, f"{tag}_ld")
    s1 = ctx.sem(f"{tag}_s1")
    s2 = ctx.sem(f"{tag}_s2")
    s3 = ctx.sem(f"{tag}_s3")
    s10, s20, s30 = s1.n, s2.n, s3.n
    ld_at = [ld.slot(i).n + 16 * (i // 2 + 1) for i in range(nt)]
    out_at = [out_pair.slot(i).n + 16 * (i // 2 + 1) for i in range(nt)]

    def gf(g):
        for i in range(nt):
            b = (i % 2) * D
            if i >= 2:
                g.wait_ge(s3.h, s30 + (i - 1))
            sl = ld.slot(i)
            sl.inc(g.dma_start(out=xs[:, b:b + D], in_=x_dram[i * 128:(i + 1) * 128, :]), 16)
            assert sl.n == ld_at[i]
    ctx.y(gf)

    def sf(s):
        for i in range(nt):
            p = (i % 2) * 4
            s.wait_ge(s1.h, s10 + i + 1)
            s2.inc(s.activation(st[:, p + 3:p + 4], st[:, p + 1:p + 2], AF.Sqrt))
    ctx.s(sf)

    def vf(v):
        for i in range(nt + 1):
            if i < nt:
                b = (i % 2) * D
                p = (i % 2) * 4
                v.wait_ge(ld.slot(i).h, ld_at[i])
                xi = xs[:, b:b + D]
                ew(ctx, v, v.reduce_sum(st[:, p:p + 1], xi, axis=AX.X))
                ew(ctx, v, v.tensor_tensor(sq[:, :], xi, xi, ALU.mult))
                ew(ctx, v, v.reduce_sum(st[:, p + 1:p + 2], sq[:, :], axis=AX.X))
                ew(ctx, v, v.tensor_scalar(st[:, p:p + 1], st[:, p:p + 1], 1.0 / D, None, ALU.mult))
                ew(ctx, v, v.tensor_tensor(st[:, p + 3:p + 4], st[:, p:p + 1], st[:, p:p + 1], ALU.mult))
                ew(ctx, v, v.tensor_scalar(st[:, p + 1:p + 2], st[:, p + 1:p + 2], 1.0 / D, None, ALU.mult))
                ew(ctx, v, v.tensor_tensor(st[:, p + 1:p + 2], st[:, p + 1:p + 2], st[:, p + 3:p + 4], ALU.subtract))
                s1.inc(v.tensor_scalar(st[:, p + 1:p + 2], st[:, p + 1:p + 2], 1e-5, None, ALU.add))
            if i >= 1:
                j = i - 1
                b = (j % 2) * D
                p = (j % 2) * 4
                v.wait_ge(s2.h, s20 + j + 1)
                if j >= 2:
                    v.wait_ge(out_pair.slot(j).h, out_at[j - 2])
                ew(ctx, v, v.reciprocal(st[:, p + 2:p + 3], st[:, p + 3:p + 4]))
                s3.inc(v.tensor_scalar(xo[:, b:b + D], xs[:, b:b + D], st[:, p:p + 1], st[:, p + 2:p + 3],
                                       ALU.subtract, ALU.mult))
    ctx.v(vf)

    def gf2(g):
        for i in range(nt):
            b = (i % 2) * D
            g.wait_ge(s3.h, s30 + i + 1)
            sl = out_pair.slot(i)
            sl.inc(g.dma_start(out=xn_dram[i * 128:(i + 1) * 128, :], in_=xo[:, b:b + D]), 16)
            assert sl.n == out_at[i]
    ctx.g(gf2)


class Proj:
    def __init__(self, ctx, tag, T, K, N, xs_dram, ws_dram, out_dram, out_mode,
                 bias_dram=None, act=None, ps=None, act_scale=1.0, w_resident=None,
                 out_pair=None, x_fm=False, share=None, colscale_dram=None):
        self.x_fm = x_fm
        self.share = share
        self.colscale = colscale_dram
        self.ctx, self.tag, self.T, self.K, self.N = ctx, tag, T, K, N
        self.xs_dram, self.ws_dram, self.out_dram = xs_dram, ws_dram, out_dram
        self.out_mode, self.bias_dram, self.act = out_mode, bias_dram, act
        self.ps, self.act_scale = ps, act_scale
        self.w_resident = w_resident
        self.out_pair = out_pair
        self.nspan = T // 512
        self.kc = [(i * 128, min(128, K - i * 128)) for i in range((K + 127) // 128)]

    def emit(self, wait=None):
        ctx, tag, N = self.ctx, self.tag, self.N
        self._wait = wait or []
        kc, nkc, nsrc = self.kc, len(self.kc), len(self.xs_dram)
        hasb = self.bias_dram is not None
        if self.out_mode == "fm":
            nper = (N + 127) // 128
        else:
            nper = 4 * ((N + 511) // 512)
        groups = [(sp, u) for sp in range(self.nspan) for u in range(nper)]

        xfm_name = self.share[0] if self.share else f"{tag}_x"
        w_name = self.share[1] if self.share else f"{tag}_w"
        if self.w_resident is None:
            wt = [ctx.sb(f"{w_name}{i}", [128, max(nkc * N, self.share[2] if self.share else 0)], BF16)
                  for i in range(nsrc)]
        else:
            wt = self.w_resident
        xfm = ctx.sb(xfm_name, [128, max(2 * nsrc * nkc * 512, self.share[3] if self.share else 0)], BF16)
        obname = f"psh_ob_{self.out_dram.dtype}" if self.share else f"{tag}_o"
        ob = ctx.sb(obname, [128, 2 * 512], self.out_dram.dtype)
        if hasb:
            brow = ctx.sb(f"{tag}_br", [128, N], BF16)
        if self.colscale is not None:
            srow = ctx.sb("psh_sr", [128, max(N, 4096)], BF16)
            srep = ctx.sb("psh_srp", [128, max(N, 4096)], BF16)
            sctmp = ctx.sb("psh_sct", [128, 2 * 512], F32)
        ones = ctx.sb("ones_t", [128, 512], BF16)
        semtag = "psh" if self.share else tag
        wl = ctx.sem(f"{semtag}_wl")
        xl = SemPair(ctx, f"{semtag}_xl")
        mm = ctx.sem(f"{semtag}_mm")
        ev = ctx.sem(f"{semtag}_ev")
        outp = self.out_pair if self.out_pair is not None else SemPair(ctx, f"{semtag}_out")
        mm0, ev0 = mm.n, ev.n
        ps = self.ps
        xl_per = nsrc * nkc
        xl_at = [xl.slot(sp).n + 16 * xl_per * (sp // 2 + 1) for sp in range(self.nspan)]
        st_at = [outp.slot(gi).n + 16 * (gi // 2 + 1) for gi in range(len(groups))]

        def xoff(buf, src, k):
            return (buf * nsrc * nkc + src * nkc + k) * 512

        def gw(g):
            for sem, val in self._wait:
                g.wait_ge(sem.h, val)
            if self.w_resident is None:
                for i in range(nsrc):
                    for j, (k0, kn) in enumerate(kc):
                        wl.inc(g.dma_start(out=wt[i][0:kn, j * N:(j + 1) * N],
                                           in_=self.ws_dram[i][k0:k0 + kn, :]), 16)
            if hasb:
                wl.inc(g.dma_start(out=brow[0:1, :], in_=self.bias_dram[:].unsqueeze(0)), 16)
            if self.colscale is not None:
                wl.inc(g.dma_start(out=srow[0:1, 0:N], in_=self.colscale[:].unsqueeze(0)), 16)
        ctx.g(gw)

        onesem = ctx.sem("ones_sem")
        def vo(v):
            onesem.inc(v.memset(ones[:, :], 1.0))
        ctx.v(vo)
        wlw, onesw = wl.n, onesem.n
        if self.colscale is not None:
            evpre = ev.n
            def tsr(t):
                t.wait_ge(wl.h, wlw)
                t.wait_ge(onesem.h, onesw)
                for ii, i0 in enumerate(range(0, N, 512)):
                    nn = min(512, N - i0)
                    if ii >= 1:
                        t.wait_ge(ev.h, evpre + ii)
                    mm.inc(t.matmul(ps[0][0:128, 0:nn], ones[0:1, 0:128], srow[0:1, i0:i0 + nn],
                                    start=True, stop=True), 1)
            ctx.t(tsr)
            def vsr(v):
                for ii, i0 in enumerate(range(0, N, 512)):
                    nn = min(512, N - i0)
                    v.wait_ge(mm.h, mm0 + ii + 1)
                    ev.inc(v.tensor_copy(srep[:, i0:i0 + nn], ps[0][0:128, 0:nn]), 1)
            ctx.v(vsr)
            mm0 = mm.n
            ev0 = ev.n

        def sx(s):
            for sem, val in self._wait:
                s.wait_ge(sem.h, val)
            for sp in range(self.nspan):
                buf = sp % 2
                if sp >= 2:
                    s.wait_ge(mm.h, mm0 + (sp - 1) * nper)
                sl = xl.slot(sp)
                for i in range(nsrc):
                    for j, (k0, kn) in enumerate(kc):
                        if self.x_fm:
                            sl.inc(s.dma_start(
                                out=xfm[0:kn, xoff(buf, i, j):xoff(buf, i, j) + 512],
                                in_=self.xs_dram[i][k0:k0 + kn, sp * 512:(sp + 1) * 512]), 16)
                        else:
                            sl.inc(s.dma_start_transpose(
                                xfm[0:kn, xoff(buf, i, j):xoff(buf, i, j) + 512],
                                self.xs_dram[i][sp * 512:(sp + 1) * 512, k0:k0 + kn]), 16)
                assert sl.n == xl_at[sp]
        ctx.y(sx)

        ev_scale_tf = 1
        ev2 = ctx.sem(f"{semtag}_ev2")
        ev20 = ev2.n
        def tf(t):
            for sem, val in self._wait:
                t.wait_ge(sem.h, val)
            t.wait_ge(wl.h, wlw)
            t.wait_ge(onesem.h, onesw)
            if self.colscale is not None:
                t.wait_ge(ev.h, ev0)
            for gi, (sp, u) in enumerate(groups):
                buf = sp % 2
                if u == 0:
                    t.wait_ge(xl.slot(sp).h, xl_at[sp])
                if gi >= 2:
                    t.wait_ge(ev.h, ev0 + ev_scale_tf * (gi - 1))
                p = ps[gi % 2]
                ins = None
                if self.out_mode == "fm":
                    n0 = u * 128
                    nn = min(128, N - n0)
                    for i in range(nsrc):
                        for j, (k0, kn) in enumerate(kc):
                            ins = t.matmul(p[0:nn, 0:512],
                                           wt[i][0:kn, j * N + n0:j * N + n0 + nn],
                                           xfm[0:kn, xoff(buf, i, j):xoff(buf, i, j) + 512],
                                           start=(i == 0 and j == 0),
                                           stop=(not hasb and i == nsrc - 1 and j == nkc - 1))
                    if hasb:
                        ins = t.matmul(p[0:nn, 0:512], brow[0:1, n0:n0 + nn], ones[0:1, 0:512],
                                       start=False, stop=True)
                else:
                    nspN = (N + 511) // 512
                    tt, nsp = u // nspN, u % nspN
                    n0 = nsp * 512
                    nn = min(512, N - n0)
                    for i in range(nsrc):
                        for j, (k0, kn) in enumerate(kc):
                            ins = t.matmul(p[0:128, 0:nn],
                                           xfm[0:kn, xoff(buf, i, j) + tt * 128:xoff(buf, i, j) + (tt + 1) * 128],
                                           wt[i][0:kn, j * N + n0:j * N + n0 + nn],
                                           start=(i == 0 and j == 0),
                                           stop=(not hasb and i == nsrc - 1 and j == nkc - 1))
                    if hasb:
                        ins = t.matmul(p[0:128, 0:nn], ones[0:1, 0:128], brow[0:1, n0:n0 + nn],
                                       start=False, stop=True)
                mm.inc(ins, 1)
        ctx.t(tf)

        def ef(e, is_act):
            for gi, (sp, u) in enumerate(groups):
                e.wait_ge(mm.h, mm0 + gi + 1)
                if gi >= 2:
                    e.wait_ge(outp.slot(gi).h, st_at[gi - 2])
                p = ps[gi % 2]
                obuf = (gi % 2) * 512
                if self.out_mode == "fm":
                    nn = min(128, N - u * 128)
                    src_ap, dst_ap = p[0:nn, 0:512], ob[0:nn, obuf:obuf + 512]
                else:
                    nspN = (N + 511) // 512
                    nn = min(512, N - (u % nspN) * 512)
                    src_ap, dst_ap = p[0:128, 0:nn], ob[0:128, obuf:obuf + nn]
                if self.colscale is not None:
                    if self.out_mode == "fm":
                        ssl = srep[0:src_ap.shape[0], u * 128:u * 128 + src_ap.shape[0]]
                        ssl = None
                        raise NotImplementedError
                    nspNc = (N + 511) // 512
                    n0c = (u % nspNc) * 512
                    nnc = src_ap.shape[1]
                    if is_act:
                        ins = e.tensor_tensor(sctmp[0:128, 0:nnc], src_ap, srep[:, n0c:n0c + nnc], ALU.mult)
                        s_eng[0].append((gi, dst_ap, nnc))
                    else:
                        ins = e.tensor_tensor(dst_ap, src_ap, srep[:, n0c:n0c + nnc], ALU.mult)
                elif is_act:
                    ins = e.activation(dst_ap, src_ap, self.act, bias=0.0, scale=self.act_scale)
                else:
                    ins = e.tensor_copy(dst_ap, src_ap)
                ev.inc(ins)
        s_eng = [[]]
        use_ev2 = self.colscale is not None and self.act is not None
        if use_ev2:
            ev_scale = 1
            def efv(v):
                for gi, (sp, u) in enumerate(groups):
                    v.wait_ge(mm.h, mm0 + gi + 1)
                    if gi >= 2:
                        v.wait_ge(outp.slot(gi).h, st_at[gi - 2])
                    p = ps[gi % 2]
                    nspNc = (N + 511) // 512
                    nnc = min(512, N - (u % nspNc) * 512)
                    n0c = (u % nspNc) * 512
                    sb = (gi % 2) * 512
                    ev.inc(v.tensor_tensor(sctmp[0:128, sb:sb + nnc],
                                           p[0:128, 0:nnc], srep[:, n0c:n0c + nnc], ALU.mult), 1)
            def efs(s):
                for gi, (sp, u) in enumerate(groups):
                    s.wait_ge(ev.h, ev0 + gi + 1)
                    obuf = (gi % 2) * 512
                    nspNc = (N + 511) // 512
                    nnc = min(512, N - (u % nspNc) * 512)
                    ev2.inc(s.activation(ob[0:128, obuf:obuf + nnc], sctmp[0:128, obuf:obuf + nnc],
                                        self.act, bias=0.0, scale=self.act_scale), 1)
            ctx.v(efv)
            ctx.s(efs)
        else:
            ev_scale = 1
            if self.act is None:
                ctx.v(lambda v: ef(v, False))
            else:
                ctx.s(lambda s: ef(s, True))

        def go(g):
            for gi, (sp, u) in enumerate(groups):
                if use_ev2:
                    g.wait_ge(ev2.h, ev20 + gi + 1)
                else:
                    g.wait_ge(ev.h, ev0 + ev_scale * (gi + 1))
                obuf = (gi % 2) * 512
                sl = outp.slot(gi)
                if self.out_mode == "fm":
                    n0 = u * 128
                    nn = min(128, N - n0)
                    sl.inc(g.dma_start(out=self.out_dram[n0:n0 + nn, sp * 512:(sp + 1) * 512],
                                       in_=ob[0:nn, obuf:obuf + 512]), 16)
                else:
                    nspN = (N + 511) // 512
                    tt, nsp = u // nspN, u % nspN
                    n0 = nsp * 512
                    nn = min(512, N - n0)
                    sl.inc(g.dma_start(out=self.out_dram[sp * 512 + tt * 128:sp * 512 + (tt + 1) * 128, n0:n0 + nn],
                                       in_=ob[0:128, obuf:obuf + nn]), 16)
                assert sl.n == st_at[gi], (gi, sl.n, st_at[gi])
        ctx.g(go)
        return outp, st_at



class Plan:

    NLANE = {"g": 6, "y": 4, "t": 1, "v": 1, "s": 1}

    def __init__(self, ctx, tag):
        self.ctx = ctx
        self.tag = tag
        self.ops = []
        self.sems = {(e, l): ctx.sem(f"{tag}_pl_{e}{l}")
                     for e in "tvsgy" for l in range(self.NLANE[e])}

    def add(self, engine, fn, deps=(), inc=1, lane=0):
        self.ops.append((engine, lane, fn, list(deps), inc))
        return len(self.ops) - 1

    def emit(self, wait=None):
        cnt = {k: s.n for k, s in self.sems.items()}
        val_after = [0] * len(self.ops)
        for i, (e, l, fn, deps, inc) in enumerate(self.ops):
            cnt[(e, l)] += inc
            val_after[i] = cnt[(e, l)]

        def body(eng_name):
            def f(eng):
                for sem, val in (wait or []):
                    eng.wait_ge(sem.h, val)
                for i, (e, l, fn, deps, inc) in enumerate(self.ops):
                    if e != eng_name:
                        continue
                    for d in deps:
                        de_, dl, _, _, _ = self.ops[d]
                        eng.wait_ge(self.sems[(de_, dl)].h, val_after[d])
                    ins = fn(eng)
                    self.sems[(e, l)].inc(ins, inc)
                    assert self.sems[(e, l)].n == val_after[i]
            return f
        for e in "tvsgy":
            if any(op[0] == e for op in self.ops):
                getattr(self.ctx, e)(body(e))
        self.val_after = val_after
        return val_after


def emit_mix(ctx, tag, xn_dram, tm_dram, out_dram, T, ps, wait=None, prev_dram=None):
    nt = T // 128
    xs = ctx.sb(f"{tag}_x", [128, 2 * D], BF16)
    sh = ctx.sb(f"{tag}_s", [128, 2 * D], BF16)
    dd = ctx.sb(f"{tag}_d", [128, D], BF16)
    oo = ctx.sb(f"{tag}_o", [128, 2 * D], BF16)
    tmr = ctx.sb(f"{tag}_tm", [128, D], BF16)
    tmw = ctx.sb(f"{tag}_tmw", [128, D], F32)
    P = Plan(ctx, tag)
    o_tm = P.add("g", lambda g: g.dma_start(out=tmw[0:1, :], in_=tm_dram[:, :]), lane=5, inc=16)
    o_tmb = P.add("v", lambda v: v.tensor_copy(tmw[1:2, :].bitcast(BF16)[:, 0:D], tmw[0:1, :]),
                  deps=[o_tm]) if False else None
    o_c = P.add("v", lambda v: v.tensor_copy(tmr[0:1, :], tmw[0:1, :]), deps=[o_tm])
    o_on = P.add("v", lambda v: v.memset(dd[:, :], 1.0))
    o_r1 = P.add("t", lambda t: t.matmul(ps[0][0:128, 0:512], dd[0:1, 0:128], tmr[0:1, 0:512],
                                         start=True, stop=True), deps=[o_c, o_on])
    o_r1e = P.add("v", lambda v: v.tensor_copy(tmr[:, 0:512], ps[0][0:128, 0:512]), deps=[o_r1])
    o_r2 = P.add("t", lambda t: t.matmul(ps[1][0:128, 0:512], dd[0:1, 0:128], tmr[0:1, 512:1024],
                                         start=True, stop=True), deps=[o_c, o_on, o_r1e])
    o_r2e = P.add("v", lambda v: v.tensor_copy(tmr[:, 512:1024], ps[1][0:128, 0:512]), deps=[o_r2])
    consts = [o_r1e, o_r2e]
    prevs = [None, None]
    lastd3 = None
    for i in range(nt):
        par = i % 2
        b = par * D
        dep_l = [x for x in [prevs[par]] if x is not None]
        lx = P.add("g", (lambda i, b: lambda g: g.dma_start(out=xs[:, b:b + D],
                                                            in_=xn_dram[i * 128:(i + 1) * 128, :]))(i, b),
                   deps=dep_l, lane=par, inc=16)
        if i == 0:
            ls = P.add("g", (lambda b: lambda g: g.dma_start(out=sh[1:128, b:b + D],
                                                             in_=xn_dram[0:127, :]))(b),
                       deps=dep_l, lane=par, inc=16)
            if prev_dram is None:
                lz = P.add("v", (lambda b: lambda v: v.memset(sh[0:1, b:b + D], 0.0))(b), deps=dep_l)
            else:
                lz = P.add("y", (lambda b: lambda g: g.dma_start(out=sh[0:1, b:b + D],
                                                                 in_=prev_dram[:, :]))(b),
                           deps=dep_l, lane=3, inc=16)
            ldep = [ls, lz]
        else:
            ls = P.add("g", (lambda i, b: lambda g: g.dma_start(out=sh[:, b:b + D],
                                                                in_=xn_dram[i * 128 - 1:(i + 1) * 128 - 1, :]))(i, b),
                       deps=dep_l, lane=par, inc=16)
            ldep = [ls]
        d1 = P.add("v", (lambda b: lambda v: v.tensor_tensor(dd[:, :], xs[:, b:b + D], sh[:, b:b + D],
                                                             ALU.subtract))(b),
                   deps=ldep + ([lastd3] if lastd3 is not None else []))
        d2 = P.add("v", (lambda b: lambda v: v.tensor_tensor(dd[:, :], dd[:, :], tmr[:, :], ALU.mult))(b),
                   deps=[d1] + consts)
        d3 = P.add("v", (lambda b: lambda v: v.tensor_tensor(oo[:, b:b + D], sh[:, b:b + D], dd[:, :],
                                                             ALU.add))(b), deps=[d2])
        st = P.add("g", (lambda i, b: lambda g: g.dma_start(out=out_dram[i * 128:(i + 1) * 128, :],
                                                            in_=oo[:, b:b + D]))(i, b),
                   deps=[d3], lane=2 + par, inc=16)
        prevs[par] = st
        lastd3 = d3
    P.emit(wait=wait)
    return P, [p for p in prevs if p is not None]


def emit_l2norm(ctx, tag, x_dram, out_dram, T, NHH, wait=None):
    W = NHH * 64
    nt = T // 128
    xs = ctx.sb(f"{tag}_x", [128, 2 * W], F32)
    sq = ctx.sb(f"{tag}_sq", [128, W], F32)
    ss = ctx.sb(f"{tag}_ss", [128, 2 * NHH], F32)
    oo = ctx.sb(f"{tag}_o", [128, 2 * W], F32)
    P = Plan(ctx, tag)
    prevs = [None, None]
    lastq2 = None
    for i in range(nt):
        par = i % 2
        b, bs = par * W, par * NHH
        dep_l = [x for x in [prevs[par]] if x is not None]
        lx = P.add("g", (lambda i, b: lambda g: g.dma_start(out=xs[:, b:b + W],
                                                            in_=x_dram[i * 128:(i + 1) * 128, :]))(i, b),
                   deps=dep_l, lane=par, inc=16)
        q1 = P.add("v", (lambda b: lambda v: v.tensor_tensor(sq[:, :], xs[:, b:b + W], xs[:, b:b + W],
                                                             ALU.mult))(b),
                   deps=[lx] + ([lastq2] if lastq2 is not None else []))
        q2 = P.add("v", (lambda bs: lambda v: v.reduce_sum(
            ss[:, bs:bs + NHH], sq[:, :].rearrange("p (h d) -> p h d", h=NHH), axis=AX.X))(bs), deps=[q1])
        q3 = P.add("s", (lambda bs: lambda s: s.activation(ss[:, bs:bs + NHH], ss[:, bs:bs + NHH],
                                                           AF.Sqrt))(bs), deps=[q2])
        q4 = P.add("v", (lambda bs: lambda v: v.tensor_scalar(ss[:, bs:bs + NHH], ss[:, bs:bs + NHH],
                                                              1e-12, None, ALU.max))(bs), deps=[q3])
        q5 = P.add("v", (lambda bs: lambda v: v.reciprocal(ss[:, bs:bs + NHH], ss[:, bs:bs + NHH]))(bs),
                   deps=[q4])
        q6 = P.add("v", (lambda b, bs: lambda v: v.tensor_tensor(
            oo[:, b:b + W], xs[:, b:b + W],
            ss[:, bs:bs + NHH].unsqueeze(2).broadcast_to((128, NHH, 64)), ALU.mult))(b, bs), deps=[q5])
        st = P.add("g", (lambda i, b: lambda g: g.dma_start(out=out_dram[i * 128:(i + 1) * 128, :],
                                                            in_=oo[:, b:b + W]))(i, b),
                   deps=[q6], lane=2 + par, inc=16)
        prevs[par] = st
        lastq2 = q2
    P.emit(wait=wait)
    return P, [p for p in prevs if p is not None]


def plan_fin(P, fs):
    return [(P.sems[(P.ops[f][0], P.ops[f][1])], P.val_after[f]) for f in fs]


def proj_fin(outp):
    return [(outp.s[0], outp.s[0].n), (outp.s[1], outp.s[1].n)]


def emit_finish_all(ctx, fins):
    def gf(g):
        for sem, val in fins:
            g.wait_ge(sem.h, val)
    ctx.g(gf)




def emit_ffn0_expert(ctx, tag, xnfull, lg0, ew, eb, ddr, ytil_out, psacc, psrot, wait=None):
    NT = 16384 // 128
    Dres = ctx.sb(f"{tag}_D", [128, 2 * 256], BF16)
    Din = ctx.sb(f"{tag}_Di", [128, 2 * 256], BF16)
    lgs = ctx.sb(f"{tag}_lg", [128, 2 * 256], F32)
    mx = ctx.sb(f"{tag}_mx", [128, 2 * 16], F32)
    xnt = ctx.sb(f"{tag}_xn", [128, 2 * 1024], BF16)
    xt = ctx.sb(f"{tag}_xt", [128, 2 * 1024], F32)
    xtb = ctx.sb(f"{tag}_xtb", [128, 2 * 1024], BF16)
    xfm = ctx.sb(f"{tag}_xfm", [128, 8 * 256], BF16)
    ews = ctx.sb(f"{tag}_ew", [128, 2 * 8 * 1024], BF16)
    ebs = ctx.sb(f"{tag}_eb", [128, 2 * 1024], BF16)
    st = ctx.sb(f"{tag}_st", [128, 16], F32)
    ob = ctx.sb(f"{tag}_ob", [128, 2 * 512], BF16)
    ones = ctx.sb(f"{tag}_on", [128, 512], BF16)
    idn = ctx.sb(f"{tag}_idn", [128, 128], F32)
    io = ctx.sb(f"{tag}_io", [128, 128], F32)
    P = Plan(ctx, tag)

    o_on = P.add("v", lambda v: v.memset(ones[:, :], 1.0))
    o_io = P.add("g", lambda g: g.iota(io[:, :], pattern=[[-1, 128]], channel_multiplier=1,
                                       allow_small_or_imprecise_dtypes=True), lane=4, inc=1)
    o_id = P.add("v", lambda v: v.tensor_scalar(idn[:, :], io[:, :], 0.0, None, ALU.is_equal), deps=[o_io])

    prev = [None, None]
    dready = []
    for i in range(NT):
        par = i % 2
        b, bm = par * 256, par * 16
        lx = P.add("g", (lambda i, b: lambda g: g.dma_start(out=lgs[:, b:b + 256],
                                                            in_=lg0[i * 128:(i + 1) * 128, :]))(i, b),
                   deps=[x for x in [prev[par]] if x is not None], lane=par, inc=16)
        m1 = P.add("v", (lambda b, bm: lambda v: v.reduce_max(
            mx[:, bm:bm + 8], lgs[:, b:b + 256].rearrange("p (n q) -> p n q", n=8), axis=AX.X))(b, bm),
            deps=[lx])
        m2 = P.add("v", (lambda b, bm: lambda v: v.tensor_tensor(
            lgs[:, b:b + 256].rearrange("p (n q) -> p n q", n=8),
            lgs[:, b:b + 256].rearrange("p (n q) -> p n q", n=8),
            mx[:, bm:bm + 8].unsqueeze(2).broadcast_to((128, 8, 32)), ALU.subtract))(b, bm), deps=[m1])
        m3 = P.add("s", (lambda i, b: lambda s: s.activation(
            Dres[:, b:b + 256], lgs[:, b:b + 256], AF.Exp))(i, b), deps=[m2] +
            ([prev[par]] if prev[par] is not None else []))
        m4 = P.add("v", (lambda b, bm: lambda v: v.reduce_sum(
            mx[:, bm + 8:bm + 16], Dres[:, b:b + 256].rearrange("p (n q) -> p n q", n=8),
            axis=AX.X))(b, bm), deps=[m3])
        m5 = P.add("v", (lambda bm: lambda v: v.reciprocal(mx[:, bm + 8:bm + 16],
                                                           mx[:, bm + 8:bm + 16]))(bm), deps=[m4])
        m6 = P.add("v", (lambda b, bm: lambda v: v.tensor_tensor(
            Dres[:, b:b + 256].rearrange("p (n q) -> p n q", n=8),
            Dres[:, b:b + 256].rearrange("p (n q) -> p n q", n=8),
            mx[:, bm + 8:bm + 16].unsqueeze(2).broadcast_to((128, 8, 32)), ALU.mult))(b, bm), deps=[m5])
        m7 = P.add("y", (lambda i, b: lambda g: g.dma_start(
            out=ddr[i * 128:(i + 1) * 128, :], in_=Dres[:, b:b + 256]))(i, b),
            deps=[m6], lane=par, inc=16)
        prev[par] = m7
        dready.append(m7)

    prevx = [None, None]
    xacc = []
    for i in range(NT):
        par = i % 2
        b = par * 1024
        lx = P.add("g", (lambda i, b: lambda g: g.dma_start(out=xnt[:, b:b + 1024],
                                                            in_=xnfull[i * 128:(i + 1) * 128, :]))(i, b),
                   deps=[x for x in [prevx[par]] if x is not None], lane=2 + par, inc=16)
        ld2 = P.add("g", (lambda i, par: lambda g: g.dma_start(
            out=Din[:, par * 256:(par + 1) * 256], in_=ddr[i * 128:(i + 1) * 128, :]))(i, par),
            deps=[dready[i]] + ([x for x in [prevx[par]] if x is not None]), lane=2 + par, inc=16)
        last = None
        for npc in range(2):
            for ds in range(2):
                last = P.add("t", (lambda i, b, npc, ds, par: lambda t: t.matmul(
                    psacc[npc * 2 + ds][0:128, 0:512],
                    Din[:, par * 256 + npc * 128:par * 256 + (npc + 1) * 128],
                    xnt[:, b + ds * 512:b + (ds + 1) * 512],
                    start=(i == 0), stop=(i == NT - 1)))(i, b, npc, ds, par),
                    deps=[lx, ld2] + ([xacc[-1]] if xacc else []))
        prevx[par] = last
        xacc.append(last)

    tdeps = []
    pnc12 = None
    for npc in range(2):
        b = npc * 1024
        e1 = P.add("v", (lambda npc, b: lambda v: v.tensor_scalar(
            xt[:, b:b + 512], psacc[npc * 2][0:128, 0:512], -33000.0, 65000.0, ALU.max, ALU.min))(npc, b),
            deps=[xacc[-1]])
        e2 = P.add("v", (lambda npc, b: lambda v: v.tensor_scalar(
            xt[:, b + 512:b + 1024], psacc[npc * 2 + 1][0:128, 0:512], -33000.0, 65000.0,
            ALU.max, ALU.min))(npc, b), deps=[xacc[-1], e1] + ([pnc12] if npc == 1 else []))
        s1 = P.add("v", (lambda npc, b: lambda v: v.reduce_sum(st[:, 0:1], xt[:, b:b + 1024], axis=AX.X))(npc, b),
                   deps=[e2])
        s2 = P.add("v", (lambda npc, b: lambda v: v.tensor_tensor(xtb[:, 0:1024].bitcast(F32)[:, 0:512] if False else xt[:, b:b + 1024], xt[:, b:b + 1024], xt[:, b:b + 1024], ALU.mult))(npc, b),
                   deps=[s1]) if False else None
        sq = ctx.sb(f"{tag}_sqt", [128, 1024], F32)
        s2 = P.add("v", (lambda npc, b, sq=sq: lambda v: v.tensor_tensor(sq[:, :], xt[:, b:b + 1024],
                                                                         xt[:, b:b + 1024], ALU.mult))(npc, b),
                   deps=[s1])
        s3 = P.add("v", (lambda sq=sq: lambda v: v.reduce_sum(st[:, 1:2], sq[:, :], axis=AX.X))(),
                   deps=[s2])
        s4 = P.add("v", lambda v: v.tensor_scalar(st[:, 0:1], st[:, 0:1], 1.0 / 1024, None, ALU.mult),
                   deps=[s3])
        s5 = P.add("v", lambda v: v.tensor_tensor(st[:, 2:3], st[:, 0:1], st[:, 0:1], ALU.mult), deps=[s4])
        s6 = P.add("v", lambda v: v.tensor_scalar(st[:, 1:2], st[:, 1:2], 1.0 / 1024, None, ALU.mult),
                   deps=[s5])
        s7 = P.add("v", lambda v: v.tensor_tensor(st[:, 1:2], st[:, 1:2], st[:, 2:3], ALU.subtract),
                   deps=[s6])
        s8 = P.add("v", lambda v: v.tensor_scalar(st[:, 1:2], st[:, 1:2], 1e-5, None, ALU.add), deps=[s7])
        s9 = P.add("s", lambda s: s.activation(st[:, 3:4], st[:, 1:2], AF.Sqrt), deps=[s8])
        s10 = P.add("v", lambda v: v.reciprocal(st[:, 2:3], st[:, 3:4]), deps=[s9])
        s11 = P.add("v", (lambda npc, b: lambda v: v.tensor_scalar(
            xt[:, b:b + 1024], xt[:, b:b + 1024], st[:, 0:1], st[:, 2:3], ALU.subtract, ALU.mult))(npc, b),
            deps=[s10])
        s12 = P.add("s", (lambda npc, b: lambda s: s.activation(
            xt[:, b:b + 1024], xt[:, b:b + 1024], AF.Tanh))(npc, b), deps=[s11])
        pnc12 = s12
        for dch in range(8):
            tr = P.add("t", (lambda npc, dch, b: lambda t: t.transpose(
                psrot[(npc * 8 + dch) % 2][0:128, 0:128], xt[:, b + dch * 128:b + (dch + 1) * 128],
                idn[:, :]))(npc, dch, b), deps=[s12, o_id] + tdeps[-2:])
            tre = P.add("v", (lambda npc, dch: lambda v: v.tensor_copy(
                xfm[:, dch * 256 + npc * 128:dch * 256 + (npc + 1) * 128],
                psrot[(npc * 8 + dch) % 2][0:128, 0:128]))(npc, dch), deps=[tr])
            tdeps.append(tre)

    prevw = [None, None]
    prevst = [None, None]
    for e in range(8):
        par = e % 2
        wb = par * 8192
        for cch in range(8):
            lw = P.add("g", (lambda e, wb, cch: lambda g: g.dma_start(
                out=ews[:, wb + cch * 1024:wb + (cch + 1) * 1024],
                in_=ew[e * 1024 + cch * 128:e * 1024 + (cch + 1) * 128, :]))(e, wb, cch),
                deps=[x for x in [prevw[par]] if x is not None], lane=par, inc=16)
        lb = P.add("g", (lambda e, par: lambda g: g.dma_start(
            out=ebs[0:1, par * 1024:(par + 1) * 1024], in_=eb[:, e * 1024:(e + 1) * 1024]))(e, par),
            deps=[x for x in [prevw[par]] if x is not None], lane=par, inc=16)
        for hs in range(2):
            gi = e * 2 + hs
            mmlast = None
            for dch in range(8):
                mmlast = P.add("t", (lambda e, wb, hs, dch, gi: lambda t: t.matmul(
                    psrot[gi % 2][0:32, 0:512],
                    xfm[:, dch * 256 + (e // 4) * 128 + (e % 4) * 32:dch * 256 + (e // 4) * 128 + (e % 4) * 32 + 32],
                    ews[:, wb + dch * 1024 + hs * 512:wb + dch * 1024 + (hs + 1) * 512],
                    start=(dch == 0), stop=False))(e, wb, hs, dch, gi),
                    deps=([lb] + tdeps[-1:] if dch == 0 else []) + ([mmlast] if mmlast else []) +
                         ([prevst[gi % 2]] if prevst[gi % 2] is not None else []))
            mmb = P.add("t", (lambda e, hs, gi, par: lambda t: t.matmul(
                psrot[gi % 2][0:32, 0:512], ones[0:1, 0:32],
                ebs[0:1, par * 1024 + hs * 512:par * 1024 + (hs + 1) * 512],
                start=False, stop=True))(e, hs, gi, par), deps=[mmlast, lb, o_on])
            ev = P.add("v", (lambda gi: lambda v: v.tensor_copy(ob[0:32, (gi % 2) * 512:(gi % 2) * 512 + 512],
                                                                psrot[gi % 2][0:32, 0:512]))(gi), deps=[mmb])
            sto = P.add("g", (lambda e, hs, gi: lambda g: g.dma_start(
                out=ytil_out[e * 32:(e + 1) * 32, hs * 512:(hs + 1) * 512],
                in_=ob[0:32, (gi % 2) * 512:(gi % 2) * 512 + 512]))(e, hs, gi),
                deps=[ev], lane=2 + gi % 2, inc=16)
            prevst[gi % 2] = sto
        prevw[par] = mmb
    P.emit(wait=wait)
    return P, [p for p in prevst if p is not None]


def emit_ffn0_tok(ctx, tag, lgfm, ytil_full, e32, e32T, gf_dram, psacc, psrot, T=2048, wait=None):
    nspan = T // 512
    Cres = ctx.sb(f"{tag}_C", [128, 16 * 512], BF16)
    lgs = ctx.sb(f"{tag}_lg", [128, 2 * 512], F32)
    e32s = ctx.sb(f"{tag}_e32", [128, 32], BF16)
    e32Ts = ctx.sb(f"{tag}_e32T", [128, 128], F32)
    dens = ctx.sb(f"{tag}_den", [128, 512], F32)
    yts = ctx.sb(f"{tag}_yt", [128, 16 * 1024], BF16)
    ob = ctx.sb(f"{tag}_ob", [128, 2 * 512], BF16)
    P = Plan(ctx, tag)

    o_e1 = P.add("g", lambda g: g.dma_start(out=e32s[:, :], in_=e32[:, :]), lane=5, inc=16)
    o_e2 = P.add("g", lambda g: g.dma_start(out=e32Ts[0:32, :], in_=e32T[:, :]), lane=5, inc=16)
    o_yt = None
    for cch in range(16):
        o_yt = P.add("g", (lambda cch: lambda g: g.dma_start(
            out=yts[:, cch * 1024:(cch + 1) * 1024],
            in_=ytil_full[cch * 128:(cch + 1) * 128, :]))(cch), lane=5, inc=16)
    prevst = [None, None]
    for sp in range(nspan):
        dlast = None
        exps = []
        for ch in range(16):
            par = ch % 2
            b = par * 512
            lx = P.add("g", (lambda sp, ch, b: lambda g: g.dma_start(
                out=lgs[:, b:b + 512], in_=lgfm[ch * 128:(ch + 1) * 128, sp * 512:(sp + 1) * 512]))(sp, ch, b),
                deps=[x for x in [exps[-2] if len(exps) >= 2 else None] if x is not None], lane=ch % 2, inc=16)
            ex = P.add("s", (lambda sp, ch, b: lambda s: s.activation(
                Cres[:, ch * 512:(ch + 1) * 512], lgs[:, b:b + 512], AF.Exp))(sp, ch, b), deps=[lx])
            dlast = P.add("t", (lambda sp, ch: lambda t: t.matmul(
                psacc[0][0:32, 0:512], e32s[:, :], Cres[:, ch * 512:(ch + 1) * 512],
                start=(ch == 0), stop=(ch == 15)))(sp, ch),
                deps=[ex, o_e1] + ([dlast] if dlast else []))
            exps.append(ex)
        dv1 = P.add("v", lambda v: v.reciprocal(dens[0:32, :], psacc[0][0:32, 0:512]), deps=[dlast])
        drep = P.add("t", lambda t: t.matmul(psacc[1][0:128, 0:512], e32Ts[0:32, :], dens[0:32, :],
                                             start=True, stop=True), deps=[dv1, o_e2])
        nlast = []
        for ch in range(16):
            nl = P.add("v", (lambda ch: lambda v: v.tensor_tensor(
                Cres[:, ch * 512:(ch + 1) * 512], Cres[:, ch * 512:(ch + 1) * 512],
                psacc[1][0:128, 0:512], ALU.mult))(ch), deps=[drep] + nlast[-1:])
            nlast.append(nl)
        for ts in range(4):
            for hs in range(2):
                gi = ts * 2 + hs
                mml = None
                for ch in range(16):
                    mml = P.add("t", (lambda sp, ts, hs, ch, gi: lambda t: t.matmul(
                        psrot[gi % 2][0:128, 0:512],
                        Cres[:, ch * 512 + ts * 128:ch * 512 + (ts + 1) * 128],
                        yts[:, ch * 1024 + hs * 512:ch * 1024 + (hs + 1) * 512],
                        start=(ch == 0), stop=(ch == 15)))(sp, ts, hs, ch, gi),
                        deps=[nlast[ch], o_yt] + ([mml] if mml else []) +
                             ([prevst[gi % 2]] if prevst[gi % 2] is not None else []))
                ev = P.add("s", (lambda gi: lambda s: s.activation(
                    ob[:, (gi % 2) * 512:(gi % 2) * 512 + 512], psrot[gi % 2][0:128, 0:512],
                    AF.Silu))(gi), deps=[mml])
                sto = P.add("g", (lambda sp, ts, hs, gi: lambda g: g.dma_start(
                    out=gf_dram[sp * 512 + ts * 128:sp * 512 + (ts + 1) * 128, hs * 512:(hs + 1) * 512],
                    in_=ob[:, (gi % 2) * 512:(gi % 2) * 512 + 512]))(sp, ts, hs, gi),
                    deps=[ev], lane=2 + gi % 2, inc=16)
                prevst[gi % 2] = sto
    P.emit(wait=wait)
    return P, [p for p in prevst if p is not None]


def emit_gnapply(ctx, tag, wkvfm, gnsb, gnw, gnb, PHc, CHMc, wkvn, psrot, T=2048, SEQ=8192, wait=None):
    P = Plan(ctx, tag)
    stt = ctx.sb(f"{tag}_st", [16, 8], F32)
    phs = ctx.sb(f"{tag}_ph", [16, 128], F32)
    chs = ctx.sb(f"{tag}_ch", [16, 8], F32)
    rhs2 = ctx.sb(f"{tag}_rh", [16, 16], F32)
    cols = ctx.sb(f"{tag}_cols", [128, 32], F32)
    xs = ctx.sb(f"{tag}_x", [128, 2 * 512], F32)
    oo = ctx.sb(f"{tag}_o", [128, 2 * 512], BF16)
    NELT = float(SEQ * 64)
    o_g = P.add("g", lambda g: g.dma_start(out=stt[:, 0:2], in_=gnsb[:, :]), lane=5, inc=16)
    o_p = P.add("g", lambda g: g.dma_start(out=phs[:, :], in_=PHc[:, :]), lane=5, inc=16)
    o_c = P.add("g", lambda g: g.dma_start(out=chs[:, :], in_=CHMc[:, :]), lane=5, inc=16)
    o_w = []
    for ch in range(8):
        o_w.append(P.add("g", (lambda ch: lambda g: g.dma_start(
            out=cols[:, 16 + ch:16 + ch + 1], in_=gnw[ch * 128:(ch + 1) * 128].unsqueeze(1)))(ch),
            lane=4, inc=16))
        o_w.append(P.add("g", (lambda ch: lambda g: g.dma_start(
            out=cols[:, 24 + ch:24 + ch + 1], in_=gnb[ch * 128:(ch + 1) * 128].unsqueeze(1)))(ch),
            lane=4, inc=16))
    a1 = P.add("v", lambda v: v.tensor_scalar(stt[:, 0:2], stt[:, 0:2], 1.0 / NELT, None, ALU.mult),
               deps=[o_g])
    a2 = P.add("v", lambda v: v.tensor_tensor(stt[:, 2:3], stt[:, 0:1], stt[:, 0:1], ALU.mult), deps=[a1])
    a3 = P.add("v", lambda v: v.tensor_tensor(stt[:, 3:4], stt[:, 1:2], stt[:, 2:3], ALU.subtract),
               deps=[a2])
    a4 = P.add("v", lambda v: v.tensor_scalar(stt[:, 3:4], stt[:, 3:4], 1e-5, None, ALU.add), deps=[a3])
    a5 = P.add("s", lambda s: s.activation(stt[:, 4:5], stt[:, 3:4], AF.Sqrt), deps=[a4])
    a6 = P.add("v", lambda v: v.reciprocal(stt[:, 5:6], stt[:, 4:5]), deps=[a5])
    b1 = P.add("v", lambda v: v.tensor_tensor(rhs2[:, 0:8], stt[:, 0:1].broadcast_to((16, 8)),
                                              chs[:, :], ALU.mult), deps=[a1, o_c])
    b2 = P.add("v", lambda v: v.tensor_tensor(rhs2[:, 8:16], stt[:, 5:6].broadcast_to((16, 8)),
                                              chs[:, :], ALU.mult), deps=[a6, o_c])
    m1 = P.add("t", lambda t: t.matmul(psrot[0][0:128, 0:16], phs[:, :], rhs2[:, 0:16],
                                       start=True, stop=True), deps=[b1, b2, o_p])
    m2 = P.add("v", lambda v: v.tensor_copy(cols[:, 0:16], psrot[0][0:128, 0:16]), deps=[m1])
    consts = [m2] + o_w
    prevst = [None, None]
    for ch in range(8):
        for sp in range(T // 512):
            gi = ch * (T // 512) + sp
            par = gi % 2
            b = par * 512
            lx = P.add("g", (lambda ch, sp, b: lambda g: g.dma_start(
                out=xs[:, b:b + 512], in_=wkvfm[ch * 128:(ch + 1) * 128, sp * 512:(sp + 1) * 512]))(ch, sp, b),
                deps=[x for x in [prevst[par]] if x is not None], lane=par, inc=16)
            c1 = P.add("v", (lambda ch, b: lambda v: v.tensor_scalar(
                xs[:, b:b + 512], xs[:, b:b + 512], cols[:, ch:ch + 1], cols[:, 8 + ch:8 + ch + 1],
                ALU.subtract, ALU.mult))(ch, b), deps=[lx] + consts)
            c2 = P.add("v", (lambda ch, b: lambda v: v.tensor_scalar(
                oo[:, b:b + 512], xs[:, b:b + 512], cols[:, 16 + ch:16 + ch + 1],
                cols[:, 24 + ch:24 + ch + 1], ALU.mult, ALU.add))(ch, b), deps=[c1])
            st = P.add("g", (lambda ch, sp, b: lambda g: g.dma_start(
                out=wkvn[ch * 128:(ch + 1) * 128, sp * 512:(sp + 1) * 512], in_=oo[:, b:b + 512]))(ch, sp, b),
                deps=[c2], lane=2 + par, inc=16)
            prevst[par] = st
    P.emit(wait=wait)
    return P, [p for p in prevst if p is not None]


def emit_combine(ctx, tag, x_dram, a_dram, b_dram, c_dram, x1_dram, T=2048, wait=None):
    P = Plan(ctx, tag)
    xs = ctx.sb(f"{tag}_x", [128, 2 * 1024], F32)
    asb = ctx.sb(f"{tag}_a", [128, 2 * 1024], F32)
    bsb = ctx.sb(f"{tag}_b", [128, 2 * 1024], BF16)
    csb = ctx.sb(f"{tag}_c", [128, 2 * 1024], BF16)
    oo = ctx.sb(f"{tag}_oo", [128, 2 * 1024], F32)
    prevst = [None, None]
    for i in range(T // 128):
        par = i % 2
        b = par * 1024
        dl = [x for x in [prevst[par]] if x is not None]
        l1 = P.add("g", (lambda i, b: lambda g: g.dma_start(out=xs[:, b:b + 1024],
                                                            in_=x_dram[i * 128:(i + 1) * 128, :]))(i, b),
                   deps=dl, lane=0 if par == 0 else 1, inc=16)
        l2 = P.add("g", (lambda i, b: lambda g: g.dma_start(out=asb[:, b:b + 1024],
                                                            in_=a_dram[i * 128:(i + 1) * 128, :]))(i, b),
                   deps=dl, lane=0 if par == 0 else 1, inc=16)
        l3 = P.add("y", (lambda i, b: lambda g: g.dma_start(out=bsb[:, b:b + 1024],
                                                            in_=b_dram[i * 128:(i + 1) * 128, :]))(i, b),
                   deps=dl, lane=par, inc=16)
        l4 = P.add("y", (lambda i, b: lambda g: g.dma_start(out=csb[:, b:b + 1024],
                                                            in_=c_dram[i * 128:(i + 1) * 128, :]))(i, b),
                   deps=dl, lane=par, inc=16)
        c1 = P.add("v", (lambda b: lambda v: v.tensor_tensor(asb[:, b:b + 1024], asb[:, b:b + 1024],
                                                             bsb[:, b:b + 1024], ALU.mult))(b),
                   deps=[l2, l4, l3])
        c2 = P.add("v", (lambda b: lambda v: v.tensor_tensor(asb[:, b:b + 1024], asb[:, b:b + 1024],
                                                             csb[:, b:b + 1024], ALU.mult))(b), deps=[c1])
        c3 = P.add("v", (lambda b: lambda v: v.tensor_tensor(oo[:, b:b + 1024], xs[:, b:b + 1024],
                                                             asb[:, b:b + 1024], ALU.add))(b), deps=[c2, l2])
        st = P.add("g", (lambda i, b: lambda g: g.dma_start(out=x1_dram[i * 128:(i + 1) * 128, :],
                                                            in_=oo[:, b:b + 1024]))(i, b),
                   deps=[c3], lane=2 + par, inc=16)
        prevst[par] = st
    P.emit(wait=wait)
    return P, [p for p in prevst if p is not None]




def _rope6(P, T_ap_re, T_ap_im, out_re, out_im, cs, sp, rsc, deps, csb=None):
    c = cs[0:64, sp * 512:(sp + 1) * 512]
    s = cs[64:128, sp * 512:(sp + 1) * 512]
    s0 = csb[0:64, sp * 512:(sp + 1) * 512]
    c64 = csb[64:128, sp * 512:(sp + 1) * 512]
    a1 = P.add("v", (lambda: lambda v: v.tensor_tensor(rsc[0:64, 0:512], T_ap_re, c, ALU.mult))(), deps=deps)
    a2 = P.add("v", (lambda: lambda v: v.tensor_tensor(rsc[0:64, 512:1024], T_ap_im, s, ALU.mult))(), deps=[a1])
    a3 = P.add("v", (lambda: lambda v: v.tensor_tensor(rsc[64:128, 0:512], T_ap_re, s0, ALU.mult))(), deps=[a2])
    a4 = P.add("v", (lambda: lambda v: v.tensor_tensor(rsc[64:128, 512:1024], T_ap_im, c64, ALU.mult))(), deps=[a3])
    a5 = P.add("v", (lambda: lambda v: v.tensor_tensor(out_re, rsc[0:64, 0:512], rsc[0:64, 512:1024],
                                                       ALU.subtract))(), deps=[a4])
    a6 = P.add("v", (lambda: lambda v: v.tensor_tensor(out_im, rsc[64:128, 0:512], rsc[64:128, 512:1024],
                                                       ALU.add))(), deps=[a5])
    return a6


def emit_l1attn(ctx, tag, qfull, zkfull, zvfull, Mpos, Msign, Vdec, cosr, sinr,
                yfm, pst, wait=None):
    NH = 4
    P = Plan(ctx, tag)
    zfm = ctx.sb(f"{tag}_zfm", [128, 2 * 8192], BF16)
    Ms = ctx.sb(f"{tag}_M", [128, 2 * 256], BF16)
    Sg = ctx.sb(f"{tag}_Sg", [128, 2 * 256], BF16)
    Vds = ctx.sb(f"{tag}_Vd", [128, 2 * 256], BF16)
    kpair = ctx.sb(f"{tag}_kp", [128, 2 * 8192], BF16)
    qpair = ctx.sb(f"{tag}_qp", [128, 2 * 8192], BF16)
    vtok = ctx.sb(f"{tag}_vt", [128, NH * 64 * 65], BF16)
    cs = ctx.sb(f"{tag}_cs", [128, 8192], BF16)
    csb = ctx.sb(f"{tag}_csb", [128, 8192], BF16)
    rsc = ctx.sb(f"{tag}_rsc", [128, 2 * 512], BF16)
    ktmp = ctx.sb(f"{tag}_ktmp", [128, 512], BF16)
    msk = ctx.sb(f"{tag}_msk", [128, 4 * 512], BF16)
    io2 = ctx.sb(f"{tag}_io2", [128, 512], F32)
    onescol = ctx.sb(f"{tag}_one", [128, 1], BF16)
    ex = ctx.sb(f"{tag}_ex", [128, 2 * 512], BF16)
    dn = ctx.sb(f"{tag}_dn", [128, 2], F32)
    dnr = ctx.sb(f"{tag}_dnr", [128, 2 * 512], F32)
    oacc = ctx.sb(f"{tag}_oacc", [128, 2 * 512], F32)
    yb = ctx.sb(f"{tag}_yb", [128, 2 * 512], BF16)

    o_io = P.add("g", lambda g: g.iota(io2[:, :], pattern=[[1, 512]], channel_multiplier=-1,
                                       allow_small_or_imprecise_dtypes=True), lane=4, inc=1)
    o_msk = []
    for mi, off in enumerate([0, 128, 256, 384]):
        o_msk.append(P.add("v", (lambda mi, off: lambda v: v.tensor_scalar(
            msk[:, mi * 512:(mi + 1) * 512], io2[:, :], float(off), None, ALU.is_ge))(mi, off),
            deps=[o_io]))
    o_one = P.add("v", lambda v: v.memset(onescol[:, :], 1.0))
    onesr = ctx.sb(f"{tag}_onesr", [128, 64], F32)
    o_oner = P.add("v", lambda v: v.memset(onesr[:, :], 1.0))
    o_m2 = P.add("g", lambda g: g.dma_start(out=Ms[:, 0:256], in_=Mpos[0:128, :]), lane=5, inc=16)
    o_m2b = P.add("g", lambda g: g.dma_start(out=Ms[0:64, 256:512], in_=Mpos[128:192, :]), lane=5, inc=16)
    o_sg = P.add("g", lambda g: g.dma_start(out=Sg[:, 0:256], in_=Msign[0:128, :]), lane=5, inc=16)
    o_sgb = P.add("g", lambda g: g.dma_start(out=Sg[0:64, 256:512], in_=Msign[128:192, :]), lane=5, inc=16)
    sg1 = P.add("v", lambda v: v.tensor_tensor(Ms[:, 0:256], Ms[:, 0:256], Sg[:, 0:256], ALU.mult),
                deps=[o_m2, o_sg])
    sg2 = P.add("v", lambda v: v.tensor_tensor(Ms[0:64, 256:512], Ms[0:64, 256:512],
                                               Sg[0:64, 256:512], ALU.mult), deps=[o_m2b, o_sgb])
    o_v1 = P.add("g", lambda g: g.dma_start(out=Vds[:, 0:256], in_=Vdec[0:128, :]), lane=5, inc=16)
    o_v2 = P.add("g", lambda g: g.dma_start(out=Vds[0:64, 256:512], in_=Vdec[128:192, :]), lane=5, inc=16)
    o_c = P.add("g", lambda g: g.dma_start(out=cs[0:64, :], in_=cosr[:, :]), lane=5, inc=16)
    o_s = P.add("g", lambda g: g.dma_start(out=cs[64:128, :], in_=sinr[:, :]), lane=5, inc=16)
    o_c2 = P.add("g", lambda g: g.dma_start(out=csb[0:64, :], in_=sinr[:, :]), lane=5, inc=16)
    o_s2 = P.add("g", lambda g: g.dma_start(out=csb[64:128, :], in_=cosr[:, :]), lane=5, inc=16)

    zk_l = []
    for sp in range(16):
        zk_l.append(P.add("y", (lambda sp: lambda g: g.dma_start_transpose(
            zfm[0:128, sp * 512:(sp + 1) * 512], zkfull[sp * 512:(sp + 1) * 512, 0:128]))(sp),
            lane=0, inc=16))
        zk_l.append(P.add("y", (lambda sp: lambda g: g.dma_start_transpose(
            zfm[0:128, 8192 + sp * 512:8192 + (sp + 1) * 512],
            zkfull[sp * 512:(sp + 1) * 512, 128:256]))(sp), lane=0, inc=16))
    zk_done = zk_l[-1]

    klast = None
    for pair in range(2):
        pb = pair * 8192
        for sp in range(16):
            evs = []
            for hh in range(2):
                h = pair * 2 + hh
                m1 = P.add("t", (lambda h, sp: lambda t: t.matmul(
                    pst[0][0:64, 0:512], Ms[:, h * 64:(h + 1) * 64],
                    zfm[0:128, sp * 512:(sp + 1) * 512], start=True, stop=False))(h, sp),
                    deps=[sg1, sg2, zk_done] + ([klast] if klast is not None else []) + evs[-1:])
                m2 = P.add("t", (lambda h, sp: lambda t: t.matmul(
                    pst[0][0:64, 0:512], Ms[0:64, 256 + h * 64:256 + (h + 1) * 64],
                    zfm[0:64, 8192 + sp * 512:8192 + (sp + 1) * 512], start=False, stop=True))(h, sp),
                    deps=[m1])
                e1 = P.add("v", (lambda hh: lambda v: v.tensor_copy(
                    ktmp[hh * 32:hh * 32 + 32, 0:512], pst[0][0:32, 0:512]))(hh), deps=[m2])
                e2 = P.add("v", (lambda hh: lambda v: v.tensor_copy(
                    ktmp[64 + hh * 32:64 + hh * 32 + 32, 0:512], pst[0][32:64, 0:512]))(hh),
                    deps=[m2, e1])
                evs += [e1, e2]
            klast = _rope6(P, ktmp[0:64, 0:512], ktmp[64:128, 0:512],
                           kpair[0:64, pb + sp * 512:pb + (sp + 1) * 512],
                           kpair[64:128, pb + sp * 512:pb + (sp + 1) * 512],
                           cs, sp, rsc, evs + [o_c, o_s, o_c2, o_s2], csb=csb)
    k_done = klast

    qlast = None
    for pair in range(2):
        pb = pair * 8192
        lq = [P.add("y", (lambda pair, pb: lambda g: g.dma_start_transpose(
            qpair[:, pb:pb + 8192], qfull[:, pair * 128:(pair + 1) * 128]))(pair, pb),
            lane=1, inc=16)]
        for sp in range(16):
            qsl_re = qpair[0:64, pb + sp * 512:pb + (sp + 1) * 512]
            qsl_im = qpair[64:128, pb + sp * 512:pb + (sp + 1) * 512]
            qlast = _rope6(P, qsl_re, qsl_im, qsl_re, qsl_im, cs, sp, rsc,
                           [lq[-1], o_c, o_s, o_c2, o_s2] + ([qlast] if qlast is not None else []) + ([k_done] if sp == 0 and pair == 0 else []), csb=csb)
    q_done = qlast

    zv_l = []
    for sp in range(16):
        zv_l.append(P.add("y", (lambda sp: lambda g: g.dma_start_transpose(
            zfm[0:128, sp * 512:(sp + 1) * 512], zvfull[sp * 512:(sp + 1) * 512, 0:128]))(sp),
            lane=2, inc=16, deps=[k_done]))
        zv_l.append(P.add("y", (lambda sp: lambda g: g.dma_start_transpose(
            zfm[0:128, 8192 + sp * 512:8192 + (sp + 1) * 512],
            zvfull[sp * 512:(sp + 1) * 512, 128:256]))(sp), lane=2, inc=16, deps=[k_done]))
    zv_done = zv_l[-1]
    vlast = None
    for h in range(NH):
        for ks in range(16):
            for kq in range(4):
                kt = ks * 4 + kq
                m1 = P.add("t", (lambda h, kt: lambda t: t.matmul(
                    pst[1][0:128, 0:64], zfm[0:128, kt * 128:(kt + 1) * 128],
                    Vds[:, h * 64:(h + 1) * 64], start=True, stop=False))(h, kt),
                    deps=[o_v1, o_v2, zv_done] + ([vlast] if vlast is not None else []))
                m2 = P.add("t", (lambda h, kt: lambda t: t.matmul(
                    pst[1][0:128, 0:64], zfm[0:64, 8192 + kt * 128:8192 + (kt + 1) * 128],
                    Vds[0:64, 256 + h * 64:256 + (h + 1) * 64], start=False, stop=True))(h, kt),
                    deps=[m1])
                vlast = P.add("v", (lambda h, kt: lambda v: v.tensor_copy(
                    vtok[:, h * 64 * 65 + kt * 65:h * 64 * 65 + kt * 65 + 64],
                    pst[1][0:128, 0:64]))(h, kt), deps=[m2])
    o_vones = P.add("v", lambda v: v.memset(
        vtok[:, :].rearrange("p (a b) -> p a b", b=65)[:, :, 64:65], 1.0),
        deps=[vlast, o_one])
    v_done = o_vones

    prevst = [None, None]
    prev_ev = {}
    for h in range(NH):
        pair, hh = h // 2, h % 2
        pb = pair * 8192
        reb, imb = hh * 32, 64 + hh * 32
        for qs in range(16):
            og = (h * 16 + qs) % 2
            ktmax = 4 * qs + 4
            avs = []
            exdeps_last = None
            for kt in range(ktmax):
                sg = kt % 2
                qsl = lambda b: qpair[b:b + 32, pb + qs * 512:pb + (qs + 1) * 512]
                ksl = lambda b: kpair[b:b + 32, pb + kt * 128:pb + (kt + 1) * 128]
                mdeps = [q_done, k_done] + ([prev_ev.get((h, qs, kt - 2))] if (h, qs, kt - 2) in prev_ev else [])
                if kt < 2 and qs == 0 and h == 0:
                    mdeps += [v_done]
                mdeps = [d for d in mdeps if d is not None]
                m1 = P.add("t", (lambda reb, kt, qs, pb, sg: lambda t: t.matmul(
                    pst[2 + sg][0:128, 0:512],
                    kpair[reb:reb + 32, pb + kt * 128:pb + (kt + 1) * 128],
                    qpair[reb:reb + 32, pb + qs * 512:pb + (qs + 1) * 512],
                    start=True, stop=False,
                    tile_position=(reb, 0) if reb >= 96 else None))(reb, kt, qs, pb, sg),
                    deps=mdeps + avs[-1:])
                m2 = P.add("t", (lambda imb, kt, qs, pb, sg: lambda t: t.matmul(
                    pst[2 + sg][0:128, 0:512],
                    kpair[imb:imb + 32, pb + kt * 128:pb + (kt + 1) * 128],
                    qpair[imb:imb + 32, pb + qs * 512:pb + (qs + 1) * 512],
                    start=False, stop=True,
                    tile_position=(imb, 0) if imb >= 96 else None))(imb, kt, qs, pb, sg),
                    deps=[m1])
                exb = (kt % 2) * 512
                exdep = [prev_ev[("ex", kt % 2)]] if ("ex", kt % 2) in prev_ev else []
                e1 = P.add("s", (lambda sg, exb: lambda s: s.activation(
                    ex[:, exb:exb + 512], pst[2 + sg][0:128, 0:512], AF.Exp,
                    bias=0.0, scale=0.125))(sg, exb), deps=[m2] + exdep)
                if kt >= 4 * qs:
                    mi = kt - 4 * qs
                    e1 = P.add("v", (lambda exb, mi: lambda v: v.tensor_tensor(
                        ex[:, exb:exb + 512], ex[:, exb:exb + 512],
                        msk[:, mi * 512:(mi + 1) * 512], ALU.mult))(exb, mi),
                        deps=[e1, o_msk[mi]])
                av = P.add("t", (lambda h, kt, exb, og: lambda t: t.matmul(
                    pst[4 + og][0:65, 0:512],
                    vtok[:, h * 64 * 65 + kt * 65:h * 64 * 65 + (kt + 1) * 65],
                    ex[:, exb:exb + 512], start=(kt == 0), stop=(kt == ktmax - 1)))(h, kt, exb, og),
                    deps=[e1, v_done] + avs[-1:] + ([prev_ev.get(("fin", og))] if ("fin", og) in prev_ev else []))
                avs.append(av)
                prev_ev[(h, qs, kt)] = e1
                prev_ev[("ex", kt % 2)] = av
            ob = og * 512
            findep = [prev_ev[("fin", og)]] if ("fin", og) in prev_ev else []
            f1 = P.add("v", (lambda og, ob: lambda v: v.tensor_copy(oacc[0:64, ob:ob + 512],
                                                                    pst[4 + og][0:64, 0:512]))(og, ob),
                       deps=[avs[-1]] + findep + ([prevst[og]] if prevst[og] is not None else []))
            f2 = P.add("v", (lambda og, ob: lambda v: v.reciprocal(dnr[0:1, ob:ob + 512],
                                                                   pst[4 + og][64:65, 0:512]))(og, ob),
                       deps=[avs[-1], f1])
            f3 = P.add("t", (lambda og, ob: lambda t: t.matmul(
                pst[6][0:64, 0:512], onesr[0:1, :],
                dnr[0:1, ob:ob + 512], start=True, stop=True))(og, ob),
                deps=([f2, o_oner, prev_ev[("rep",)]] if ("rep",) in prev_ev else [f2, o_oner]))
            f4 = P.add("v", (lambda og, ob: lambda v: v.tensor_tensor(
                yb[0:64, ob:ob + 512], oacc[0:64, ob:ob + 512], pst[6][0:64, 0:512],
                ALU.mult))(og, ob), deps=[f3, f1])
            prev_ev[("rep",)] = f4
            prev_ev[("fin", og)] = f4
            sto = P.add("g", (lambda h, qs, ob: lambda g: g.dma_start(
                out=yfm[h * 64:(h + 1) * 64, qs * 512:(qs + 1) * 512], in_=yb[0:64, ob:ob + 512]))(h, qs, ob),
                deps=[f4] + ([prevst[og]] if prevst[og] is not None else []), lane=og, inc=16)
            prevst[og] = sto
    P.emit(wait=wait)
    return P, [p for p in prevst if p is not None]




def emit_route1(ctx, tag, rl, dw, T=2048, wait=None):
    P = Plan(ctx, tag)
    E = 8
    ls = ctx.sb(f"{tag}_l", [128, 2 * E], F32)
    sc = ctx.sb(f"{tag}_s", [128, 2 * 16], F32)
    t1 = ctx.sb(f"{tag}_t1", [128, 2 * E], F32)
    t2 = ctx.sb(f"{tag}_t2", [128, 2 * E], F32)
    oo = ctx.sb(f"{tag}_o", [128, 2 * E], F32)
    prevst = [None, None]
    for i in range(T // 128):
        par = i % 2
        b, bs = par * E, par * 16
        lx = P.add("g", (lambda i, b: lambda g: g.dma_start(out=ls[:, b:b + E],
                                                            in_=rl[i * 128:(i + 1) * 128, :]))(i, b),
                   deps=[x for x in [prevst[par]] if x is not None], lane=par, inc=16)
        L = lambda sl: ls[:, b:b + E]
        a1 = P.add("v", (lambda b, bs: lambda v: v.reduce_max(sc[:, bs:bs + 1], ls[:, b:b + E], axis=AX.X))(b, bs), deps=[lx])
        a2 = P.add("v", (lambda b, bs: lambda v: v.tensor_scalar(t1[:, b:b + E], ls[:, b:b + E], sc[:, bs:bs + 1], None, ALU.is_equal))(b, bs), deps=[a1])
        a3 = P.add("s", (lambda b, bs: lambda s: s.activation(sc[:, bs + 2:bs + 2 + E], ls[:, b:b + E], AF.Abs))(b, bs), deps=[a1])
        a4 = P.add("v", (lambda b, bs: lambda v: v.tensor_scalar(sc[:, bs + 2:bs + 2 + E], sc[:, bs + 2:bs + 2 + E], sc[:, bs:bs + 1], None, ALU.max))(b, bs), deps=[a3])
        a5 = P.add("v", (lambda b, bs: lambda v: v.reciprocal(sc[:, bs + 2:bs + 2 + E], sc[:, bs + 2:bs + 2 + E]))(b, bs), deps=[a4])
        a6 = P.add("v", (lambda b, bs: lambda v: v.tensor_scalar(t2[:, b:b + E], ls[:, b:b + E], sc[:, bs:bs + 1], -1.0, ALU.subtract, ALU.mult))(b, bs), deps=[a2, a5])
        a7 = P.add("v", (lambda b, bs: lambda v: v.tensor_tensor(t2[:, b:b + E], t2[:, b:b + E], sc[:, bs + 2:bs + 2 + E], ALU.mult))(b, bs), deps=[a6])
        a8 = P.add("v", (lambda b: lambda v: v.tensor_scalar(t2[:, b:b + E], t2[:, b:b + E], 0.02, None, ALU.is_le))(b), deps=[a7])
        a9 = P.add("v", (lambda b, bs: lambda v: v.tensor_scalar(sc[:, bs + 2:bs + 2 + E], ls[:, b:b + E], sc[:, bs:bs + 1], None, ALU.subtract))(b, bs), deps=[a8, a4])
        a10 = P.add("s", (lambda bs: lambda s: s.activation(sc[:, bs + 2:bs + 2 + E], sc[:, bs + 2:bs + 2 + E], AF.Exp))(bs), deps=[a9])
        a11 = P.add("v", (lambda b, bs: lambda v: v.tensor_tensor(sc[:, bs + 2:bs + 2 + E], sc[:, bs + 2:bs + 2 + E], t2[:, b:b + E], ALU.mult))(b, bs), deps=[a10])
        a12 = P.add("v", (lambda bs: lambda v: v.reduce_sum(sc[:, bs + 1:bs + 2], sc[:, bs + 2:bs + 2 + E], axis=AX.X))(bs), deps=[a11])
        a13 = P.add("v", (lambda b, bs: lambda v: v.tensor_tensor(sc[:, bs + 2:bs + 2 + E], sc[:, bs + 2:bs + 2 + E], t1[:, b:b + E], ALU.mult))(b, bs), deps=[a12])
        a14 = P.add("v", (lambda bs: lambda v: v.reduce_sum(sc[:, bs + 10:bs + 11], sc[:, bs + 2:bs + 2 + E], axis=AX.X))(bs), deps=[a13])
        a15 = P.add("v", (lambda bs: lambda v: v.reciprocal(sc[:, bs + 1:bs + 2], sc[:, bs + 1:bs + 2]))(bs), deps=[a14])
        a16 = P.add("v", (lambda bs: lambda v: v.tensor_tensor(sc[:, bs + 10:bs + 11], sc[:, bs + 10:bs + 11], sc[:, bs + 1:bs + 2], ALU.mult))(bs), deps=[a15])
        b1 = P.add("v", (lambda b: lambda v: v.tensor_scalar(t1[:, b:b + E], t1[:, b:b + E], -1e38, None, ALU.mult))(b), deps=[a13])
        b2 = P.add("v", (lambda b: lambda v: v.tensor_tensor(t1[:, b:b + E], t1[:, b:b + E], ls[:, b:b + E], ALU.add))(b), deps=[b1])
        b3 = P.add("v", (lambda b, bs: lambda v: v.reduce_max(sc[:, bs + 11:bs + 12], t1[:, b:b + E], axis=AX.X))(b, bs), deps=[b2])
        b4 = P.add("v", (lambda b, bs: lambda v: v.tensor_scalar(oo[:, b:b + E], t1[:, b:b + E], sc[:, bs + 11:bs + 12], None, ALU.is_equal))(b, bs), deps=[b3])
        b5 = P.add("s", (lambda b, bs: lambda s: s.activation(sc[:, bs + 2:bs + 2 + E], ls[:, b:b + E], AF.Abs))(b, bs), deps=[b4, a16])
        b6 = P.add("v", (lambda b, bs: lambda v: v.tensor_scalar(sc[:, bs + 2:bs + 2 + E], sc[:, bs + 2:bs + 2 + E], sc[:, bs + 11:bs + 12], None, ALU.max))(b, bs), deps=[b5])
        b7 = P.add("v", (lambda b, bs: lambda v: v.reciprocal(sc[:, bs + 2:bs + 2 + E], sc[:, bs + 2:bs + 2 + E]))(b, bs), deps=[b6])
        b8 = P.add("v", (lambda b, bs: lambda v: v.tensor_scalar(t2[:, b:b + E], ls[:, b:b + E], sc[:, bs + 11:bs + 12], -1.0, ALU.subtract, ALU.mult))(b, bs), deps=[b7])
        b9 = P.add("v", (lambda b, bs: lambda v: v.tensor_tensor(t2[:, b:b + E], t2[:, b:b + E], sc[:, bs + 2:bs + 2 + E], ALU.mult))(b, bs), deps=[b8])
        b10 = P.add("v", (lambda b: lambda v: v.tensor_scalar(t2[:, b:b + E], t2[:, b:b + E], 0.02, None, ALU.is_le))(b), deps=[b9])
        b11 = P.add("v", (lambda b, bs: lambda v: v.tensor_scalar(sc[:, bs + 2:bs + 2 + E], t1[:, b:b + E], sc[:, bs + 11:bs + 12], None, ALU.subtract))(b, bs), deps=[b10])
        b12 = P.add("s", (lambda bs: lambda s: s.activation(sc[:, bs + 2:bs + 2 + E], sc[:, bs + 2:bs + 2 + E], AF.Exp))(bs), deps=[b11])
        b13 = P.add("v", (lambda b, bs: lambda v: v.tensor_tensor(sc[:, bs + 2:bs + 2 + E], sc[:, bs + 2:bs + 2 + E], t2[:, b:b + E], ALU.mult))(b, bs), deps=[b12])
        b14 = P.add("v", (lambda bs: lambda v: v.reduce_sum(sc[:, bs + 12:bs + 13], sc[:, bs + 2:bs + 2 + E], axis=AX.X))(bs), deps=[b13])
        b15 = P.add("v", (lambda b, bs: lambda v: v.tensor_tensor(sc[:, bs + 2:bs + 2 + E], sc[:, bs + 2:bs + 2 + E], oo[:, b:b + E], ALU.mult))(b, bs), deps=[b14])
        b16 = P.add("v", (lambda bs: lambda v: v.reduce_sum(sc[:, bs + 13:bs + 14], sc[:, bs + 2:bs + 2 + E], axis=AX.X))(bs), deps=[b15])
        b17 = P.add("v", (lambda bs: lambda v: v.reciprocal(sc[:, bs + 12:bs + 13], sc[:, bs + 12:bs + 13]))(bs), deps=[b16])
        b18 = P.add("v", (lambda bs: lambda v: v.tensor_tensor(sc[:, bs + 13:bs + 14], sc[:, bs + 13:bs + 14], sc[:, bs + 12:bs + 13], ALU.mult))(bs), deps=[b17])
        c1 = P.add("v", (lambda b, bs: lambda v: v.tensor_scalar(t1[:, b:b + E], ls[:, b:b + E], sc[:, bs:bs + 1], None, ALU.is_equal))(b, bs), deps=[b18, b2])
        c2 = P.add("v", (lambda b, bs: lambda v: v.tensor_scalar(t1[:, b:b + E], t1[:, b:b + E], sc[:, bs + 10:bs + 11], None, ALU.mult))(b, bs), deps=[c1])
        c3 = P.add("v", (lambda b, bs: lambda v: v.tensor_scalar(oo[:, b:b + E], oo[:, b:b + E], sc[:, bs + 13:bs + 14], None, ALU.mult))(b, bs), deps=[c2])
        c4 = P.add("v", (lambda b: lambda v: v.tensor_tensor(oo[:, b:b + E], oo[:, b:b + E], t1[:, b:b + E], ALU.add))(b), deps=[c3])
        st = P.add("g", (lambda i, b: lambda g: g.dma_start(out=dw[i * 128:(i + 1) * 128, :],
                                                            in_=oo[:, b:b + E]))(i, b),
                   deps=[c4], lane=2 + par, inc=16)
        prevst[par] = st
    P.emit(wait=wait)
    return P, [p for p in prevst if p is not None]


def emit_route3(ctx, tag, rl, dw, T=2048, wait=None):
    P = Plan(ctx, tag)
    E = 16
    ls = ctx.sb(f"{tag}_l", [128, 2 * E], F32)
    sc = ctx.sb(f"{tag}_s", [128, 2 * 8], F32)
    t1 = ctx.sb(f"{tag}_t1", [128, 2 * E], F32)
    oo = ctx.sb(f"{tag}_o", [128, 2 * E], F32)
    prevst = [None, None]
    for i in range(T // 128):
        par = i % 2
        b, bs = par * E, par * 8
        lx = P.add("g", (lambda i, b: lambda g: g.dma_start(out=ls[:, b:b + E],
                                                            in_=rl[i * 128:(i + 1) * 128, :]))(i, b),
                   deps=[x for x in [prevst[par]] if x is not None], lane=par, inc=16)
        a1 = P.add("v", (lambda b, bs: lambda v: v.reduce_max(sc[:, bs:bs + 1], ls[:, b:b + E], axis=AX.X))(b, bs), deps=[lx])
        a2 = P.add("v", (lambda b, bs: lambda v: v.tensor_scalar(t1[:, b:b + E], ls[:, b:b + E], sc[:, bs:bs + 1], None, ALU.is_equal))(b, bs), deps=[a1])
        a3 = P.add("v", (lambda b: lambda v: v.tensor_scalar(oo[:, b:b + E], t1[:, b:b + E], -1e38, None, ALU.mult))(b), deps=[a2])
        a4 = P.add("v", (lambda b: lambda v: v.tensor_tensor(oo[:, b:b + E], oo[:, b:b + E], ls[:, b:b + E], ALU.add))(b), deps=[a3])
        a5 = P.add("v", (lambda b, bs: lambda v: v.reduce_max(sc[:, bs + 1:bs + 2], oo[:, b:b + E], axis=AX.X))(b, bs), deps=[a4])
        a6 = P.add("v", (lambda b, bs: lambda v: v.tensor_scalar(oo[:, b:b + E], oo[:, b:b + E], sc[:, bs + 1:bs + 2], None, ALU.is_equal))(b, bs), deps=[a5])
        a7 = P.add("v", (lambda bs: lambda v: v.tensor_scalar(sc[:, bs + 2:bs + 3], sc[:, bs + 1:bs + 2], sc[:, bs:bs + 1], None, ALU.subtract))(bs), deps=[a6])
        a8 = P.add("s", (lambda bs: lambda s: s.activation(sc[:, bs + 2:bs + 3], sc[:, bs + 2:bs + 3], AF.Exp))(bs), deps=[a7])
        a9 = P.add("v", (lambda bs: lambda v: v.tensor_scalar(sc[:, bs + 3:bs + 4], sc[:, bs + 2:bs + 3], 1.0, None, ALU.add))(bs), deps=[a8])
        a10 = P.add("v", (lambda bs: lambda v: v.reciprocal(sc[:, bs + 3:bs + 4], sc[:, bs + 3:bs + 4]))(bs), deps=[a9])
        a11 = P.add("v", (lambda bs: lambda v: v.tensor_tensor(sc[:, bs + 2:bs + 3], sc[:, bs + 2:bs + 3], sc[:, bs + 3:bs + 4], ALU.mult))(bs), deps=[a10])
        a12 = P.add("v", (lambda b, bs: lambda v: v.tensor_scalar(t1[:, b:b + E], t1[:, b:b + E], sc[:, bs + 3:bs + 4], None, ALU.mult))(b, bs), deps=[a11])
        a13 = P.add("v", (lambda b, bs: lambda v: v.tensor_scalar(oo[:, b:b + E], oo[:, b:b + E], sc[:, bs + 2:bs + 3], None, ALU.mult))(b, bs), deps=[a12])
        a14 = P.add("v", (lambda b: lambda v: v.tensor_tensor(oo[:, b:b + E], oo[:, b:b + E], t1[:, b:b + E], ALU.add))(b), deps=[a13])
        st = P.add("g", (lambda i, b: lambda g: g.dma_start(out=dw[i * 128:(i + 1) * 128, :],
                                                            in_=oo[:, b:b + E]))(i, b),
                   deps=[a14], lane=2 + par, inc=16)
        prevst[par] = st
    P.emit(wait=wait)
    return P, [p for p in prevst if p is not None]


def emit_moe(ctx, tag, xn, wg, wu, wd, dwd, out, E, FD, gated, psA, psB, psC, psD,
             T=2048, wait=None):
    P = Plan(ctx, tag)
    nkc = D // 128
    nfc = FD // 128
    nspan = T // 512
    xfm = ctx.sb(f"{tag}_x", [128, nkc * 512], BF16)
    wgs = ctx.sb(f"{tag}_wg", [128, nkc * FD], BF16)
    wus = ctx.sb(f"{tag}_wu", [128, nkc * FD], BF16) if gated else None
    wds = ctx.sb(f"{tag}_wd", [128, nfc * 1024], BF16)
    hs = ctx.sb(f"{tag}_h", [128, nfc * 512], BF16)
    gtmp = ctx.sb(f"{tag}_gt", [128, 512], BF16)
    dws = ctx.sb(f"{tag}_dw", [128, (T // 128) * E], F32)
    zz = ctx.sb(f"{tag}_z", [128, 1024], F32)
    ob = ctx.sb(f"{tag}_ob", [128, 2 * 512], F32)
    o_z = P.add("v", lambda v: v.memset(zz[:, :], 0.0))
    zs = []
    for i in range(T // 128):
        zs.append(P.add("g", (lambda i: lambda g: g.dma_start(
            out=out[i * 128:(i + 1) * 128, :], in_=zz[:, :]))(i), deps=[o_z], lane=3, inc=16))
    o_d = []
    for i in range(T // 128):
        o_d.append(P.add("g", (lambda i: lambda g: g.dma_start(
            out=dws[:, i * E:(i + 1) * E], in_=dwd[i * 128:(i + 1) * 128, :]))(i), lane=5, inc=16,
            deps=[o_d[-1]] if o_d else []))
    zdone = zs[-1]
    last_store = {}
    prev_wuse = None
    prev_xuse = None
    laste = None
    for e in range(E):
        wl = []
        wdep = [prev_wuse] if prev_wuse is not None else []
        for j in range(nkc):
            wl.append(P.add("g", (lambda e, j: lambda g: g.dma_start(
                out=wgs[:, j * FD:(j + 1) * FD],
                in_=wg[e * D + j * 128:e * D + (j + 1) * 128, :]))(e, j), deps=wdep, lane=0, inc=16))
            if gated:
                wl.append(P.add("g", (lambda e, j: lambda g: g.dma_start(
                    out=wus[:, j * FD:(j + 1) * FD],
                    in_=wu[e * D + j * 128:e * D + (j + 1) * 128, :]))(e, j), deps=wdep, lane=0, inc=16))
        for j in range(nfc):
            wl.append(P.add("g", (lambda e, j: lambda g: g.dma_start(
                out=wds[:, j * 1024:(j + 1) * 1024],
                in_=wd[e * FD + j * 128:e * FD + (j + 1) * 128, :]))(e, j), deps=wdep, lane=0, inc=16))
        w_done = wl[-1]
        for sp in range(nspan):
            xdep = [prev_xuse] if prev_xuse is not None else []
            xl = []
            for j in range(nkc):
                xl.append(P.add("y", (lambda sp, j: lambda g: g.dma_start_transpose(
                    xfm[:, j * 512:(j + 1) * 512],
                    xn[sp * 512:(sp + 1) * 512, j * 128:(j + 1) * 128]))(sp, j),
                    deps=xdep, lane=sp % 2, inc=16))
            x_done = xl[-1]
            hdone = []
            mms = []
            for fc in range(nfc):
                m1 = None
                for j in range(nkc):
                    m1 = P.add("t", (lambda j, fc: lambda t: t.matmul(
                        psA[0:128, 0:512],
                        wgs[:, j * FD + fc * 128:j * FD + (fc + 1) * 128],
                        xfm[:, j * 512:(j + 1) * 512],
                        start=(j == 0), stop=(j == nkc - 1)))(j, fc),
                        deps=[w_done, x_done] + ([m1] if m1 else []) + hdone[-1:] +
                             ([laste] if laste is not None and fc == 0 else []))
                mms.append(m1)
                if gated:
                    m2 = None
                    for j in range(nkc):
                        m2 = P.add("t", (lambda j, fc: lambda t: t.matmul(
                            psB[0:128, 0:512],
                            wus[:, j * FD + fc * 128:j * FD + (fc + 1) * 128],
                            xfm[:, j * 512:(j + 1) * 512],
                            start=(j == 0), stop=(j == nkc - 1)))(j, fc),
                            deps=[m1] + ([m2] if m2 else []))
                    s1 = P.add("s", (lambda: lambda s: s.activation(
                        gtmp[:, 0:512], psA[0:128, 0:512], AF.Silu))(), deps=[m1, m2] + hdone[-1:])
                    s2 = P.add("v", (lambda fc: lambda v: v.tensor_tensor(
                        hs[:, fc * 512:(fc + 1) * 512], gtmp[:, 0:512], psB[0:128, 0:512],
                        ALU.mult))(fc), deps=[s1])
                else:
                    s2 = P.add("s", (lambda fc: lambda s: s.activation(
                        hs[:, fc * 512:(fc + 1) * 512], psA[0:128, 0:512], AF.Silu))(fc),
                        deps=[m1] + hdone[-1:])
                hdone.append(s2)
            prev_xuse = mms[-1]
            for tt in range(4):
                ti = sp * 4 + tt
                for ns in range(2):
                    m3 = None
                    for fc in range(nfc):
                        m3 = P.add("t", (lambda tt, ns, fc: lambda t: t.matmul(
                            psC[0:128, 0:512],
                            hs[:, fc * 512 + tt * 128:fc * 512 + (tt + 1) * 128],
                            wds[:, fc * 1024 + ns * 512:fc * 1024 + (ns + 1) * 512],
                            start=(fc == 0), stop=(fc == nfc - 1)))(tt, ns, fc),
                            deps=hdone + ([m3] if m3 else []) + ([laste] if laste is not None else []))
                    e1 = P.add("v", (lambda ti, ns, e: lambda v: v.tensor_scalar(
                        ob[:, (ns % 2) * 512:(ns % 2) * 512 + 512], psC[0:128, 0:512],
                        dws[:, ti * E + e:ti * E + e + 1], None, ALU.mult))(ti, ns, e),
                        deps=[m3, o_d[ti]] + ([laste] if laste is not None else []))
                    laste = e1
                    sdep = [e1, zdone] + ([last_store[(ti, ns)]] if (ti, ns) in last_store else [])
                    st = P.add("g", (lambda ti, ns: lambda g: g.dma_start(
                        out=out[ti * 128:(ti + 1) * 128, ns * 512:(ns + 1) * 512],
                        in_=ob[:, (ns % 2) * 512:(ns % 2) * 512 + 512],
                        accum_op=ALU.add))(ti, ns), deps=sdep, lane=1 + (ti % 2) * 1, inc=16)
                    last_store[(ti, ns)] = st
                    laste = st
        prev_wuse = laste
    sts = list(last_store.values())
    P.emit(wait=wait)
    return P, sts[-2:]


def emit_gatemul(ctx, tag, a_dram, b_dram, out_dram, T, N, wait=None):
    P = Plan(ctx, tag)
    CW = 1024
    asb = ctx.sb(f"{tag}_a", [128, CW], a_dram.dtype)
    bsb = ctx.sb(f"{tag}_b", [128, CW], b_dram.dtype)
    ga = ctx.sb(f"{tag}_g", [128, CW], BF16)
    prevst = [None, None]
    for i in range(T // 128):
        for nch in range(N // CW):
            n0 = nch * CW
            dl = [x for x in [prevst[0]] if x is not None]
            l1 = P.add("g", (lambda i, n0: lambda g: g.dma_start(out=asb[:, 0:CW],
                                                                 in_=a_dram[i * 128:(i + 1) * 128, n0:n0 + CW]))(i, n0),
                       deps=dl, lane=0, inc=16)
            l2 = P.add("y", (lambda i, n0: lambda g: g.dma_start(out=bsb[:, 0:CW],
                                                                 in_=b_dram[i * 128:(i + 1) * 128, n0:n0 + CW]))(i, n0),
                       deps=dl, lane=0, inc=16)
            c1 = P.add("s", (lambda: lambda s: s.activation(ga[:, 0:CW], asb[:, 0:CW], AF.Silu))(),
                       deps=[l1])
            c2 = P.add("v", (lambda: lambda v: v.tensor_tensor(ga[:, 0:CW], ga[:, 0:CW],
                                                               bsb[:, 0:CW], ALU.mult))(), deps=[c1, l2])
            st = P.add("g", (lambda i, n0: lambda g: g.dma_start(out=out_dram[i * 128:(i + 1) * 128, n0:n0 + CW],
                                                                 in_=ga[:, 0:CW]))(i, n0),
                       deps=[c2], lane=2, inc=16)
            prevst[0] = st
    P.emit(wait=wait)
    return P, [p for p in prevst if p is not None]


def emit_combine2(ctx, tag, x_dram, a_dram, b1_dram, x1_dram, T=2048, b2_dram=None, wait=None):
    P = Plan(ctx, tag)
    xs = ctx.sb(f"{tag}_x", [128, 1024], F32)
    asb = ctx.sb(f"{tag}_a", [128, 1024], F32)
    bsb = ctx.sb(f"{tag}_b", [128, 1024], F32)
    b2sb = ctx.sb(f"{tag}_b2", [128, 1024], F32) if b2_dram is not None else None
    oo = ctx.sb(f"{tag}_oo", [128, 1024], F32)
    prevst = [None, None]
    for i in range(T // 128):
        par = 0
        b = 0
        dl = [x for x in [prevst[par]] if x is not None]
        l1 = P.add("g", (lambda i, b: lambda g: g.dma_start(out=xs[:, b:b + 1024],
                                                            in_=x_dram[i * 128:(i + 1) * 128, :]))(i, b),
                   deps=dl, lane=4 + par, inc=16)
        l2 = P.add("g", (lambda i, b: lambda g: g.dma_start(out=asb[:, b:b + 1024],
                                                            in_=a_dram[i * 128:(i + 1) * 128, :]))(i, b),
                   deps=dl, lane=par, inc=16)
        l3 = P.add("y", (lambda i, b: lambda g: g.dma_start(out=bsb[:, b:b + 1024],
                                                            in_=b1_dram[i * 128:(i + 1) * 128, :]))(i, b),
                   deps=dl, lane=par, inc=16)
        lls = [l2, l3]
        if b2_dram is not None:
            l4 = P.add("y", (lambda i, b: lambda g: g.dma_start(out=b2sb[:, b:b + 1024],
                                                                in_=b2_dram[i * 128:(i + 1) * 128, :]))(i, b),
                       deps=dl, lane=2 + par, inc=16)
            c0 = P.add("v", (lambda b: lambda v: v.tensor_tensor(bsb[:, b:b + 1024], bsb[:, b:b + 1024],
                                                                 b2sb[:, b:b + 1024], ALU.add))(b),
                       deps=[l3, l4])
            lls = [l2, c0]
        c1 = P.add("s", (lambda b: lambda s: s.activation(bsb[:, b:b + 1024], bsb[:, b:b + 1024],
                                                          AF.Silu))(b), deps=lls[1:])
        c2 = P.add("v", (lambda b: lambda v: v.tensor_tensor(asb[:, b:b + 1024], asb[:, b:b + 1024],
                                                             bsb[:, b:b + 1024], ALU.mult))(b),
                   deps=[c1, l2])
        c3 = P.add("v", (lambda b: lambda v: v.tensor_tensor(oo[:, b:b + 1024], xs[:, b:b + 1024],
                                                             asb[:, b:b + 1024], ALU.add))(b), deps=[c2, l2, l1])
        st = P.add("g", (lambda i, b: lambda g: g.dma_start(out=x1_dram[i * 128:(i + 1) * 128, :],
                                                            in_=oo[:, b:b + 1024]))(i, b),
                   deps=[c3], lane=2 + par, inc=16)
        prevst[par] = st
    P.emit(wait=wait)
    return P, [p for p in prevst if p is not None]




def emit_l2flash(ctx, tag, qfm, kfull, vfull, uqp, uqm, kdp, kdm, ofm, pst, wait=None):
    T = 2048
    P = Plan(ctx, tag)
    qh = ctx.sb(f"{tag}_q", [128, 2 * T], BF16)
    kfm = ctx.sb(f"{tag}_k", [128, 2 * 512], BF16)
    vsp = ctx.sb(f"{tag}_v", [128, 4 * 4 * 65], BF16)
    uq = ctx.sb(f"{tag}_uq", [128, 2 * T], F32)
    kdc = ctx.sb(f"{tag}_kd", [128, 2 * 64], F32)
    kdr = ctx.sb(f"{tag}_kdr", [128, 128], F32)
    eq = ctx.sb(f"{tag}_eq", [128, 2 * 512], F32)
    dct = ctx.sb(f"{tag}_dc", [128, 2 * 512], F32)
    dsc = ctx.sb(f"{tag}_ds", [128, 2 * 512], F32)
    ex = ctx.sb(f"{tag}_ex", [128, 2 * 512], BF16)
    exf = ctx.sb(f"{tag}_exf", [128, 2 * 512], F32)
    dnr = ctx.sb(f"{tag}_dnr", [128, 2 * 512], F32)
    oacc = ctx.sb(f"{tag}_oacc", [128, 2 * 512], F32)
    yb = ctx.sb(f"{tag}_yb", [128, 2 * 512], BF16)
    onesr = ctx.sb(f"{tag}_onesr", [128, 128], F32)
    idn = ctx.sb(f"{tag}_idn", [128, 128], F32)
    io = ctx.sb(f"{tag}_io", [128, 128], F32)

    o_io = P.add("g", lambda g: g.iota(io[:, :], pattern=[[-1, 128]], channel_multiplier=1,
                                       allow_small_or_imprecise_dtypes=True), lane=4, inc=1)
    o_id = P.add("v", lambda v: v.tensor_scalar(idn[:, :], io[:, :], 0.0, None, ALU.is_equal), deps=[o_io])
    o_on = P.add("v", lambda v: v.memset(onesr[:, :], 1.0))
    o_u1 = P.add("g", lambda g: g.dma_start(out=uq[0:1, 0:T], in_=uqp[:, :]), lane=5, inc=16)
    o_u2 = P.add("g", lambda g: g.dma_start(out=uq[0:1, T:2 * T], in_=uqm[:, :]), lane=5, inc=16)
    o_k1 = P.add("g", lambda g: g.dma_start(out=kdr[0:64, :], in_=kdm[:, :].rearrange("o (a b) -> (o a) b", a=64)),
                 lane=5, inc=16)
    o_t1 = P.add("t", lambda t: t.transpose(pst[0][0:128, 0:64], kdr[0:64, :], idn[0:64, 0:64]),
                 deps=[o_k1, o_id])
    o_e1 = P.add("v", lambda v: v.tensor_copy(kdc[:, 0:64], pst[0][0:128, 0:64]), deps=[o_t1])
    o_k2 = P.add("g", lambda g: g.dma_start(out=kdr[0:64, :], in_=kdp[:, :].rearrange("o (a b) -> (o a) b", a=64)),
                 lane=5, inc=16, deps=[o_t1])
    o_t2 = P.add("t", lambda t: t.transpose(pst[0][0:128, 0:64], kdr[0:64, :], idn[0:64, 0:64]),
                 deps=[o_k2, o_e1])
    o_e2 = P.add("v", lambda v: v.tensor_copy(kdc[:, 64:128], pst[0][0:128, 0:64]), deps=[o_t2])

    prevst = [None, None]
    prev = {}
    for hg in range(4):
        lqs = []
        for cc in range(2):
            lqs.append(P.add("y", (lambda hg, cc: lambda g: g.dma_start(
                out=qh[:, cc * T:(cc + 1) * T], in_=qfm[hg * 256 + cc * 128:hg * 256 + (cc + 1) * 128, :]))(hg, cc),
                lane=3, inc=16, deps=[prev[("q",)]] if ("q",) in prev else []))
        q_done = lqs[-1]
        for qs in range(4):
            e1 = P.add("t", (lambda qs: lambda t: t.matmul(
                pst[0][0:128, 0:512], onesr[0:1, 0:128], uq[0:1, qs * 512:(qs + 1) * 512],
                start=True, stop=True))(qs), deps=[o_u1, o_on, o_e1, o_e2] + ([prev[("eq",)]] if ("eq",) in prev else []))
            e2 = P.add("v", lambda v: v.tensor_copy(eq[:, 0:512], pst[0][0:128, 0:512]), deps=[e1])
            e3 = P.add("t", (lambda qs: lambda t: t.matmul(
                pst[0][0:128, 0:512], onesr[0:1, 0:128], uq[0:1, T + qs * 512:T + (qs + 1) * 512],
                start=True, stop=True))(qs), deps=[o_u2, e2])
            e4 = P.add("v", lambda v: v.tensor_copy(eq[:, 512:1024], pst[0][0:128, 0:512]), deps=[e3])
            prev[("eq",)] = e4
            og_used = []
            avl = {0: [], 1: [], 2: [], 3: []}
            exl = []
            for ks in range(16):
                lk = []
                ldep = [prev[("kl",)]] if ("kl",) in prev else []
                for cc in range(2):
                    lk.append(P.add("y", (lambda hg, ks, cc: lambda g: g.dma_start_transpose(
                        kfm[:, cc * 512:(cc + 1) * 512],
                        kfull[ks * 512:(ks + 1) * 512, hg * 256 + cc * 128:hg * 256 + (cc + 1) * 128]))(hg, ks, cc),
                        lane=ks % 2, inc=16, deps=ldep))
                lv = []
                for kt in range(4):
                    lv.append(P.add("g", (lambda hg, ks, kt: lambda g: g.dma_start(
                        out=vsp[:, kt * 260:kt * 260 + 260].rearrange("p (h c) -> p h c", h=4)[:, :, 0:64],
                        in_=vfull[ks * 512 + kt * 128:ks * 512 + (kt + 1) * 128,
                                  hg * 256:(hg + 1) * 256].rearrange("p (h c) -> p h c", h=4)))(hg, ks, kt),
                        lane=ks % 2, inc=16, deps=ldep))
                lo = P.add("v", (lambda: lambda v: v.memset(
                    vsp[:, :].rearrange("p (a c) -> p a c", c=65)[:, :, 64:65], 1.0))(), deps=lv)
                kv_done = [lk[-1], lo]
                for kt in range(4):
                    dpar = kt % 2
                    db = dpar * 512
                    ddep = [prev[("dc", dpar)]] if ("dc", dpar) in prev else []
                    d1 = P.add("v", (lambda ks, kt, db: lambda v: v.tensor_scalar(
                        dct[:, db:db + 512], eq[:, 0:512], kdc[:, ks * 4 + kt:ks * 4 + kt + 1],
                        None, ALU.mult))(ks, kt, db), deps=[e2, o_e1] + ddep)
                    d2 = P.add("v", (lambda ks, kt, db: lambda v: v.tensor_scalar(
                        dsc[:, db:db + 512], eq[:, 512:1024], kdc[:, 64 + ks * 4 + kt:64 + ks * 4 + kt + 1],
                        None, ALU.mult))(ks, kt, db), deps=[e4, o_e2, d1])
                    d3 = P.add("v", (lambda db: lambda v: v.tensor_tensor(
                        dct[:, db:db + 512], dct[:, db:db + 512], dsc[:, db:db + 512], ALU.min))(db),
                        deps=[d2])
                    prev[("dc", dpar)] = d3
                    for h in range(4):
                        cc, rb = h // 2, (h % 2) * 64
                        sg = (kt * 4 + h) % 2
                        m1 = P.add("t", (lambda cc, rb, ks, kt, qs, sg: lambda t: t.matmul(
                            pst[2 + sg][0:128, 0:512],
                            kfm[rb:rb + 64, cc * 512 + kt * 128:cc * 512 + (kt + 1) * 128],
                            qh[rb:rb + 64, cc * T + qs * 512:cc * T + (qs + 1) * 512],
                            start=True, stop=True))(cc, rb, ks, kt, qs, sg),
                            deps=kv_done + [q_done] + ([prev[("sc", sg)]] if ("sc", sg) in prev else []))
                        x1 = P.add("v", (lambda sg, db: lambda v: v.tensor_tensor(
                            exf[:, sg * 512:(sg + 1) * 512], pst[2 + sg][0:128, 0:512],
                            dct[:, db:db + 512], ALU.mult))(sg, db),
                            deps=[m1, d3] + ([prev[("exf", sg)]] if ("exf", sg) in prev else []))
                        prev[("sc", sg)] = x1
                        x2 = P.add("s", (lambda sg: lambda s: s.activation(
                            ex[:, sg * 512:(sg + 1) * 512], exf[:, sg * 512:(sg + 1) * 512],
                            AF.Exp))(sg), deps=[x1] + ([prev[("ex", sg)]] if ("ex", sg) in prev else []))
                        prev[("exf", sg)] = x2
                        av = P.add("t", (lambda h, kt, sg, ks: lambda t: t.matmul(
                            pst[4 + h][0:65, 0:512],
                            vsp[:, kt * 260 + h * 65:kt * 260 + (h + 1) * 65],
                            ex[:, sg * 512:(sg + 1) * 512],
                            start=(ks == 0 and kt == 0), stop=(ks == 15 and kt == 3)))(h, kt, sg, ks),
                            deps=[x2] + (avl[h][-1:]) + ([prev[("fin", h)]] if ("fin", h) in prev and ks == 0 and kt == 0 else []))
                        prev[("ex", sg)] = av
                        avl[h].append(av)
                prev[("kl",)] = avl[3][-1]
            for h in range(4):
                pg = pst[4 + h]
                rb = 0
                ob = (h % 2) * 512
                f1 = P.add("v", (lambda pg, rb, ob: lambda v: v.tensor_copy(
                    oacc[0:64, ob:ob + 512], pg[rb:rb + 64, 0:512]))(pg, rb, ob),
                    deps=[avl[h][-1]] + ([prev[("f1", h % 2)]] if ("f1", h % 2) in prev else []))
                f2 = P.add("v", (lambda pg, rb, ob: lambda v: v.reciprocal(
                    dnr[0:1, ob:ob + 512], pg[rb + 64:rb + 65, 0:512]))(pg, rb, ob), deps=[f1])
                f3 = P.add("t", (lambda ob: lambda t: t.matmul(
                    pst[1][0:64, 0:512], onesr[0:1, 0:64], dnr[0:1, ob:ob + 512],
                    start=True, stop=True))(ob), deps=[f2] + ([prev[("f3",)]] if ("f3",) in prev else []))
                f4 = P.add("v", (lambda ob: lambda v: v.tensor_tensor(
                    yb[0:64, ob:ob + 512], oacc[0:64, ob:ob + 512], pst[1][0:64, 0:512],
                    ALU.mult))(ob), deps=[f3])
                prev[("f3",)] = f4
                prev[("fin", h)] = f4
                prev[("f1", h % 2)] = f4
                sto = P.add("g", (lambda hg, h, qs, ob: lambda g: g.dma_start(
                    out=ofm[(hg * 4 + h) * 64:(hg * 4 + h + 1) * 64, qs * 512:(qs + 1) * 512],
                    in_=yb[0:64, ob:ob + 512]))(hg, h, qs, ob),
                    deps=[f4] + ([prevst[h % 2]] if prevst[h % 2] is not None else []), lane=2 + h % 2, inc=16)
                prevst[h % 2] = sto
        prev[("q",)] = prevst[0]
    P.emit(wait=wait)
    return P, [p for p in prevst if p is not None]


def emit_l3a(ctx, tag, qfm, kfm_d, vtok_d, Wf_d, bf_d, qf_out, kvpart, pst, wait=None):
    T = 2048
    P = Plan(ctx, tag)
    wfs = ctx.sb(f"{tag}_wf", [128, 64], BF16)
    bfc = ctx.sb(f"{tag}_bf", [128, 1], F32)
    bfr = ctx.sb(f"{tag}_bfr", [128, 64], F32)
    qsl = ctx.sb(f"{tag}_q", [128, 2 * 512], BF16)
    ksl = ctx.sb(f"{tag}_k", [128, 2 * 512], BF16)
    kft = ctx.sb(f"{tag}_kf", [128, 2 * 64], BF16)
    vt = ctx.sb(f"{tag}_vt", [128, 2 * 65], BF16)
    qfb = ctx.sb(f"{tag}_qfb", [128, 2 * 512], BF16)
    kvb = ctx.sb(f"{tag}_kv", [128, 2 * 65], F32)
    onesr = ctx.sb(f"{tag}_on", [128, 128], F32)
    o_w = P.add("g", lambda g: g.dma_start(out=wfs[0:64, :], in_=Wf_d[:, :]), lane=5, inc=16)
    o_b = P.add("g", lambda g: g.dma_start(out=bfc[0:64, :], in_=bf_d[:].unsqueeze(1)), lane=5, inc=16,
                deps=[o_w])
    o_on = P.add("v", lambda v: v.memset(onesr[:, :], 1.0))
    o_br = P.add("t", lambda t: t.matmul(pst[6][0:128, 0:64], onesr[0:64, 0:1],
                                         bf_d[:].unsqueeze(0) if False else bfc[0:64, 0:1],
                                         start=True, stop=True), deps=[o_b, o_on]) if False else None
    bfrow = ctx.sb(f"{tag}_bfw", [128, 64], F32)
    o_b2 = P.add("g", lambda g: g.dma_start(out=bfrow[0:1, :], in_=bf_d[:].unsqueeze(0)), lane=5, inc=16,
                 deps=[o_b])
    o_br = P.add("t", lambda t: t.matmul(pst[6][0:128, 0:64], onesr[0:1, 0:128], bfrow[0:1, :],
                                         start=True, stop=True), deps=[o_b2, o_on])
    o_bre = P.add("v", lambda v: v.tensor_copy(bfr[:, :], pst[6][0:128, 0:64]), deps=[o_br])
    prev = {}
    prevst = [None, None]
    lvprev = None
    for h in range(16):
        for sp in range(4):
            par = sp % 2
            lq = P.add("y", (lambda h, sp, par: lambda g: g.dma_start(
                out=qsl[0:64, par * 512:(par + 1) * 512],
                in_=qfm[h * 64:(h + 1) * 64, sp * 512:(sp + 1) * 512]))(h, sp, par),
                lane=par, inc=16, deps=[prev[("q", par)]] if ("q", par) in prev else [])
            m1 = P.add("t", (lambda par: lambda t: t.matmul(
                pst[0 + par][0:64, 0:512], wfs[0:64, :], qsl[0:64, par * 512:(par + 1) * 512],
                start=True, stop=True))(par), deps=[lq, o_w] + ([prev[("qm", par)]] if ("qm", par) in prev else []))
            s1 = P.add("s", (lambda par: lambda s: s.activation(
                qfb[0:64, par * 512:(par + 1) * 512], pst[0 + par][0:64, 0:512], AF.Relu,
                bias=bfc[0:64, :], scale=1.0))(par), deps=[m1] + ([prev[("qs", par)]] if ("qs", par) in prev else []))
            prev[("qm", par)] = s1
            st = P.add("g", (lambda h, sp, par: lambda g: g.dma_start(
                out=qf_out[h * 64:(h + 1) * 64, sp * 512:(sp + 1) * 512],
                in_=qfb[0:64, par * 512:(par + 1) * 512]))(h, sp, par),
                lane=par, inc=16, deps=[s1])
            prev[("q", par)] = st
            prev[("qs", par)] = st
        kvmm = None
        for ti in range(16):
            par = ti % 2
            lk = P.add("y", (lambda h, ti, par: lambda g: g.dma_start(
                out=ksl[0:64, par * 512:par * 512 + 128],
                in_=kfm_d[h * 64:(h + 1) * 64, ti * 128:(ti + 1) * 128]))(h, ti, par),
                lane=2 + par, inc=16, deps=[prev[("k", par)]] if ("k", par) in prev else [])
            lv = P.add("g", (lambda h, ti, par: lambda g: g.dma_start(
                out=vt[:, par * 65:par * 65 + 64],
                in_=vtok_d[ti * 128:(ti + 1) * 128, h * 64:(h + 1) * 64]))(h, ti, par),
                lane=4, inc=16,
                deps=([lvprev] if lvprev is not None else []) +
                     ([prev[("v", par)]] if ("v", par) in prev else []))
            lvprev = lv
            lo = P.add("v", (lambda par: lambda v: v.memset(vt[:, par * 65 + 64:par * 65 + 65], 1.0))(par),
                       deps=[lv])
            m1 = P.add("t", (lambda par: lambda t: t.matmul(
                pst[2][0:128, 0:64], ksl[0:64, par * 512:par * 512 + 128], wfs[0:64, :],
                start=True, stop=True))(par), deps=[lk, o_w] + ([prev[("km",)]] if ("km",) in prev else []))
            v1 = P.add("v", (lambda par: lambda v: v.tensor_tensor(
                kft[:, par * 64:(par + 1) * 64], pst[2][0:128, 0:64], bfr[:, :], ALU.add))(par),
                deps=[m1, o_bre] + ([prev[("kf", par)]] if ("kf", par) in prev else []))
            v2 = P.add("v", (lambda par: lambda v: v.tensor_scalar(
                kft[:, par * 64:(par + 1) * 64], kft[:, par * 64:(par + 1) * 64], 0.0, None,
                ALU.max))(par), deps=[v1])
            prev[("km",)] = v2
            kvmm = P.add("t", (lambda par, ti: lambda t: t.matmul(
                pst[3][0:64, 0:65], kft[:, par * 64:(par + 1) * 64], vt[:, par * 65:(par + 1) * 65],
                start=(ti == 0), stop=(ti == 15)))(par, ti),
                deps=[v2, lo] + ([kvmm] if kvmm else []))
            prev[("kf", par)] = kvmm
            prev[("k", par)] = m1
            prev[("v", par)] = kvmm
        ev = P.add("v", (lambda h: lambda v: v.tensor_copy(kvb[0:64, (h % 2) * 65:(h % 2) * 65 + 65],
                                                           pst[3][0:64, 0:65]))(h),
                   deps=[kvmm] + ([prev[("kve", h % 2)]] if ("kve", h % 2) in prev else []))
        st = P.add("g", (lambda h: lambda g: g.dma_start(
            out=kvpart[h * 64:(h + 1) * 64, :], in_=kvb[0:64, (h % 2) * 65:(h % 2) * 65 + 65]))(h),
            deps=[ev], lane=2 + h % 2, inc=16)
        prev[("kve", h % 2)] = st
        prevst[h % 2] = st
    P.emit(wait=wait)
    return P, [p for p in prevst if p is not None]


def emit_l3b(ctx, tag, kvparts, qffm, yfm, pst, wait=None):
    T = 2048
    P = Plan(ctx, tag)
    kvs = ctx.sb(f"{tag}_kv", [128, 8 * 65], F32)
    kvt = ctx.sb(f"{tag}_kvt", [128, 65], F32)
    kva = ctx.sb(f"{tag}_kva", [128, 8 * 65], F32)
    qfs = ctx.sb(f"{tag}_qf", [128, 2 * 512], BF16)
    kvbf = ctx.sb(f"{tag}_kvb", [128, 8 * 65], BF16)
    dnr = ctx.sb(f"{tag}_dnr", [128, 2 * 512], F32)
    oacc = ctx.sb(f"{tag}_oacc", [128, 2 * 512], F32)
    yb = ctx.sb(f"{tag}_yb", [128, 2 * 512], BF16)
    onesr = ctx.sb(f"{tag}_on", [128, 64], F32)
    o_on = P.add("v", lambda v: v.memset(onesr[:, :], 1.0))
    sums = []
    l0prev = None
    for ch in range(8):
        l0 = P.add("g", (lambda ch: lambda g: g.dma_start(
            out=kva[:, ch * 65:(ch + 1) * 65], in_=kvparts[ch * 128:(ch + 1) * 128, :]))(ch),
            lane=0, inc=16, deps=[l0prev] if l0prev is not None else [])
        l0prev = l0
        prevs = l0
        for r in range(1, 8):
            dl = [prevs] + ([sums[-1]] if (r == 1 and sums) else [])
            lr = P.add("g", (lambda ch, r: lambda g: g.dma_start(
                out=kvt[:, :], in_=kvparts[r * 1024 + ch * 128:r * 1024 + (ch + 1) * 128, :]))(ch, r),
                lane=1, inc=16, deps=dl)
            prevs = P.add("v", (lambda ch: lambda v: v.tensor_tensor(
                kva[:, ch * 65:(ch + 1) * 65], kva[:, ch * 65:(ch + 1) * 65], kvt[:, :], ALU.add))(ch),
                deps=[lr, l0])
        sums.append(P.add("v", (lambda ch: lambda v: v.tensor_copy(
            kvbf[:, ch * 65:(ch + 1) * 65], kva[:, ch * 65:(ch + 1) * 65]))(ch), deps=[prevs]))
    kv_done = sums
    prev = {}
    prevst = [None, None]
    for h in range(16):
        cc, rb = h // 2, (h % 2) * 64
        for sp in range(4):
            par = sp % 2
            lq = P.add("y", (lambda h, sp, par, rb: lambda g: g.dma_start(
                out=qfs[rb:rb + 64, par * 512:(par + 1) * 512],
                in_=qffm[h * 64:(h + 1) * 64, sp * 512:(sp + 1) * 512]))(h, sp, par, rb),
                lane=par, inc=16, deps=[prev[("q", par)]] if ("q", par) in prev else [])
            m1 = P.add("t", (lambda cc, rb, par: lambda t: t.matmul(
                pst[0 + par][0:64, 0:512], kvbf[rb:rb + 64, cc * 65:cc * 65 + 64],
                qfs[rb:rb + 64, par * 512:(par + 1) * 512], start=True, stop=True))(cc, rb, par),
                deps=[lq] + kv_done + ([prev[("m", par)]] if ("m", par) in prev else []))
            m2 = P.add("t", (lambda cc, rb, par: lambda t: t.matmul(
                pst[2][0:1, 0:512], kvbf[rb:rb + 64, cc * 65 + 64:cc * 65 + 65],
                qfs[rb:rb + 64, par * 512:(par + 1) * 512], start=True, stop=True))(cc, rb, par),
                deps=[m1] + ([prev[("m2",)]] if ("m2",) in prev else []))
            f1 = P.add("v", (lambda par: lambda v: v.tensor_scalar(
                dnr[0:1, par * 512:(par + 1) * 512], pst[2][0:1, 0:512], 1e-6, None, ALU.max))(par),
                deps=[m2] + ([prev[("f1", par)]] if ("f1", par) in prev else []))
            f2 = P.add("v", (lambda par: lambda v: v.reciprocal(
                dnr[0:1, par * 512:(par + 1) * 512], dnr[0:1, par * 512:(par + 1) * 512]))(par), deps=[f1])
            prev[("m2",)] = f2
            f3 = P.add("t", (lambda par: lambda t: t.matmul(
                pst[3][0:64, 0:512], onesr[0:1, 0:64], dnr[0:1, par * 512:(par + 1) * 512],
                start=True, stop=True))(par), deps=[f2, o_on] + ([prev[("f3",)]] if ("f3",) in prev else []))
            f4 = P.add("v", (lambda par: lambda v: v.tensor_copy(
                oacc[0:64, par * 512:(par + 1) * 512], pst[0 + par][0:64, 0:512]))(par),
                deps=[m1] + ([prev[("f4", par)]] if ("f4", par) in prev else []))
            prev[("m", par)] = f4
            f5 = P.add("v", (lambda par: lambda v: v.tensor_tensor(
                yb[0:64, par * 512:(par + 1) * 512], oacc[0:64, par * 512:(par + 1) * 512],
                pst[3][0:64, 0:512], ALU.mult))(par), deps=[f3, f4])
            prev[("f3",)] = f5
            prev[("f1", par)] = f5
            prev[("f4", par)] = f5
            st = P.add("g", (lambda h, sp, par: lambda g: g.dma_start(
                out=yfm[h * 64:(h + 1) * 64, sp * 512:(sp + 1) * 512],
                in_=yb[0:64, par * 512:(par + 1) * 512]))(h, sp, par),
                deps=[f5] + ([prevst[par]] if prevst[par] is not None else []), lane=2 + par, inc=16)
            prev[("q", par)] = st
            prevst[par] = st
    P.emit(wait=wait)
    return P, [p for p in prevst if p is not None]




def emit_scan(ctx, tag, rfm, krows, vrows, deceta, wkvfm, gnstats,
              psA, psB, psC, NH=4, C=128, SEQ=8192, wait=None):
    nc = ctx.nc
    nchunk = SEQ // C
    H64 = NH * 64

    ST = ctx.sb(f"{tag}_ST", [64, H64], F32)
    kfm = ctx.sb(f"{tag}_kfm", [64, C], F32)
    rfm_s = ctx.sb(f"{tag}_rfm", [64, 2 * NH * C], F32)
    krw = ctx.sb(f"{tag}_krw", [128, 2 * H64], F32)
    vrw = ctx.sb(f"{tag}_vrw", [128, 2 * H64], F32)
    Lm = ctx.sb(f"{tag}_L", [128, 128], F32)
    LTm = ctx.sb(f"{tag}_LT", [128, 128], F32)
    Gs = ctx.sb(f"{tag}_G", [128, 128], F32)
    GrT = ctx.sb(f"{tag}_GrT", [128, 128], F32)
    pwP = ctx.sb(f"{tag}_pwP", [64, 2 * 64], F32)
    pwT = ctx.sb(f"{tag}_pwT", [64, 6 * 64], F32)
    Xs = ctx.sb(f"{tag}_X", [128, 64], F32)
    X2 = ctx.sb(f"{tag}_X2", [64, 128], F32)
    Es = ctx.sb(f"{tag}_E", [128, 64], F32)
    wkv_s = ctx.sb(f"{tag}_wkv", [64, 2 * C], F32)
    qr_s = ctx.sb(f"{tag}_qr", [64, C], F32)
    ksc = ctx.sb(f"{tag}_ksc", [128, 64], F32)
    sq_s = ctx.sb(f"{tag}_sqs", [64, C], F32)
    md = ctx.sb(f"{tag}_md", [128, NH * 128], F32)
    mdT = ctx.sb(f"{tag}_mdT", [128, NH * 128], F32)
    mdrT = ctx.sb(f"{tag}_mdrT", [128, NH * 128], F32)
    powB = ctx.sb(f"{tag}_powB", [64, NH * C], F32)
    pcol = ctx.sb(f"{tag}_pcol", [128, 8 * NH], F32)
    rows = ctx.sb(f"{tag}_rows", [1, NH * 8 * C], F32)
    nsqc = ctx.sb(f"{tag}_nsqc", [1, 8 * NH], F32)
    stat = ctx.sb(f"{tag}_stat", [64, 2 * NH], F32)
    phic = ctx.sb(f"{tag}_phic", [128, 4], F32)
    sc1 = ctx.sb(f"{tag}_sc1", [128, 4], F32)
    io1 = ctx.sb(f"{tag}_io1", [128, 128], F32)
    iop = ctx.sb(f"{tag}_iop", [128, 4], F32)
    ident = ctx.sb(f"{tag}_idn", [128, 128], F32)
    mks = ctx.sb(f"{tag}_mks", [128, 3 * 128], F32)
    de = ctx.sb(f"{tag}_de", [1, 4 * NH], F32)
    der = ctx.sb(f"{tag}_der", [128, 4 * NH], F32)
    onesf = ctx.sb(f"{tag}_onesf", [128, 128], F32)
    stato = ctx.sb(f"{tag}_stato", [1, 2 * NH], F32)
    ppc = ctx.sb(f"{tag}_ppc", [64, 1], F32)

    P = Plan(ctx, tag)

    o_de = P.add("g", lambda g: g.dma_start(out=de[0:1, 0:2 * NH], in_=deceta[:, :]), lane=5, inc=16)
    o_io = P.add("g", lambda g: g.iota(io1[:, :], pattern=[[-1, 128]], channel_multiplier=1,
                                       allow_small_or_imprecise_dtypes=True), lane=4, inc=1)
    o_iop = P.add("g", lambda g: g.iota(iop[:, 0:1], pattern=[[0, 1]], channel_multiplier=1,
                                        allow_small_or_imprecise_dtypes=True), lane=4, inc=1)
    o_ones = P.add("v", lambda v: v.memset(onesf[:, :], 1.0))
    o_st0 = P.add("v", lambda v: v.memset(ST[:, :], 0.0))
    o_stat0 = P.add("v", lambda v: v.memset(stat[:, :], 0.0))
    o_nsq0 = P.add("v", lambda v: v.memset(nsqc[0:1, :], 0.0))
    o_id = P.add("v", lambda v: v.tensor_scalar(ident[:, :], io1[:, :], 0.0, None, ALU.is_equal), deps=[o_io])
    o_mk0 = P.add("v", lambda v: v.tensor_scalar(mks[:, 0:128], io1[:, :], 0.0, None, ALU.is_gt), deps=[o_io])
    o_mk1 = P.add("v", lambda v: v.tensor_scalar(mks[:, 128:256], io1[:, :], 0.0, None, ALU.is_lt), deps=[o_io])
    o_mk2 = P.add("v", lambda v: v.tensor_scalar(mks[:, 256:384], io1[:, :], 0.0, None, ALU.is_le), deps=[o_io])
    o_iorev = P.add("v", lambda v: v.tensor_scalar(iop[:, 1:2], iop[:, 0:1], -1.0, float(C - 1),
                                                   ALU.mult, ALU.add), deps=[o_iop])
    o_sig = P.add("s", lambda s: s.activation(de[0:1, 0:2 * NH], de[0:1, 0:2 * NH], AF.Sigmoid), deps=[o_de])
    o_ln = P.add("s", lambda s: s.activation(de[0:1, 2 * NH:3 * NH], de[0:1, 0:NH], AF.Ln), deps=[o_sig])
    o_rep = P.add("t", lambda t: t.matmul(psA[0:128, 0:3 * NH], onesf[0:1, 0:128], de[0:1, 0:3 * NH],
                                          start=True, stop=True), deps=[o_ln, o_ones])
    o_repe = P.add("v", lambda v: v.tensor_copy(der[:, 0:3 * NH], psA[0:128, 0:3 * NH]), deps=[o_rep])
    o_neg = []
    for h in range(NH):
        o_neg.append(P.add("v", (lambda h: lambda v: v.tensor_scalar(
            der[:, 3 * NH + h:3 * NH + h + 1], der[:, 2 * NH + h:2 * NH + h + 1], -1.0, None, ALU.mult))(h),
            deps=[o_repe]))

    head_consts_done = []
    for h in range(NH):
        M = h * 128
        Pc = h * 8
        R = h * 8 * C
        lnc = der[:, 2 * NH + h:2 * NH + h + 1]
        nlnc = der[:, 3 * NH + h:3 * NH + h + 1]
        etc = der[:, NH + h:NH + h + 1]
        dcc = der[:, h:h + 1]
        dn = [o_repe, o_neg[h], o_mk0, o_mk1, o_mk2]
        a1 = P.add("s", (lambda h, M, lnc, nlnc: lambda s: s.activation(
            md[:, M:M + 128], io1[:, :], AF.Exp, bias=nlnc, scale=lnc))(h, M, lnc, nlnc), deps=dn)
        a2 = P.add("v", (lambda h, M, etc: lambda v: v.tensor_tensor(
            md[:, M:M + 128], md[:, M:M + 128], mks[:, 0:128], ALU.mult))(h, M, etc), deps=[a1])
        a3 = P.add("v", (lambda h, M, etc: lambda v: v.tensor_scalar(
            md[:, M:M + 128], md[:, M:M + 128], etc, None, ALU.mult))(h, M, etc), deps=[a2])
        b1 = P.add("s", (lambda h, M, nlnc: lambda s: s.activation(
            mdT[:, M:M + 128], io1[:, :], AF.Exp, bias=nlnc, scale=nlnc))(h, M, nlnc), deps=dn)
        b2 = P.add("v", (lambda h, M: lambda v: v.tensor_tensor(
            mdT[:, M:M + 128], mdT[:, M:M + 128], mks[:, 128:256], ALU.mult))(h, M), deps=[b1])
        b3 = P.add("v", (lambda h, M, etc: lambda v: v.tensor_scalar(
            mdT[:, M:M + 128], mdT[:, M:M + 128], etc, None, ALU.mult))(h, M, etc), deps=[b2])
        c1 = P.add("s", (lambda h, M, nlnc: lambda s: s.activation(
            mdrT[:, M:M + 128], io1[:, :], AF.Exp, bias=0.0, scale=nlnc))(h, M, nlnc), deps=dn)
        c2 = P.add("v", (lambda h, M: lambda v: v.tensor_tensor(
            mdrT[:, M:M + 128], mdrT[:, M:M + 128], mks[:, 256:384], ALU.mult))(h, M), deps=[c1])
        c3 = P.add("v", (lambda h, M, etc: lambda v: v.tensor_scalar(
            mdrT[:, M:M + 128], mdrT[:, M:M + 128], etc, None, ALU.mult))(h, M, etc), deps=[c2])
        d1 = P.add("s", (lambda h, Pc, lnc: lambda s: s.activation(
            pcol[:, Pc:Pc + 1], iop[:, 0:1], AF.Exp, bias=0.0, scale=lnc))(h, Pc, lnc),
            deps=dn + [o_iop])
        d2 = P.add("s", (lambda h, Pc, lnc: lambda s: s.activation(
            pcol[:, Pc + 1:Pc + 2], iop[:, 1:2], AF.Exp, bias=0.0, scale=lnc))(h, Pc, lnc),
            deps=dn + [o_iorev])
        d3 = P.add("v", (lambda h, Pc, etc: lambda v: v.tensor_scalar(
            pcol[:, Pc + 1:Pc + 2], pcol[:, Pc + 1:Pc + 2], etc, None, ALU.mult))(h, Pc, etc), deps=[d2])
        d4 = P.add("v", (lambda h, Pc, dcc, etc: lambda v: v.tensor_scalar(
            pcol[:, Pc + 2:Pc + 3], der[:, h:h + 1], etc, -2.0, ALU.mult, ALU.mult))(h, Pc, dcc, etc),
            deps=[o_repe])
        d5 = P.add("v", (lambda h, Pc, etc: lambda v: v.tensor_scalar(
            pcol[:, Pc + 3:Pc + 4], der[:, NH + h:NH + h + 1], etc, None, ALU.mult))(h, Pc, etc),
            deps=[o_repe])
        d6 = P.add("v", (lambda h, Pc: lambda v: v.tensor_scalar(
            pcol[:, Pc + 4:Pc + 5], onesf[:, 0:1], float(C), None, ALU.mult))(h, Pc), deps=[o_ones])
        d7 = P.add("s", (lambda h, Pc, lnc: lambda s: s.activation(
            pcol[:, Pc + 4:Pc + 5], pcol[:, Pc + 4:Pc + 5], AF.Exp, bias=0.0, scale=lnc))(h, Pc, lnc),
            deps=[d6] + dn)
        head_consts_done.append([a3, b3, c3, d1, d3, d4, d5, d7])
    o_iorow = P.add("g", lambda g: g.iota(Gs[0:64, 0:128], pattern=[[1, 128]], channel_multiplier=0,
                                          allow_small_or_imprecise_dtypes=True), lane=4, inc=1)
    for h in range(NH):
        lnc = der[:, 2 * NH + h:2 * NH + h + 1]
        e1 = P.add("s", (lambda h, lnc: lambda s: s.activation(
            powB[0:64, h * C:(h + 1) * C], Gs[0:64, 0:128], AF.Exp,
            bias=lnc[0:64, :], scale=lnc[0:64, :]))(h, lnc), deps=[o_iorow, o_repe, o_neg[h]])
        head_consts_done[h].append(e1)
        e2 = P.add("v", (lambda h: lambda v: v.memset(rows[0:1, h * 8 * C + 5 * C:h * 8 * C + 6 * C], 1.0))(h))
        e3 = P.add("v", (lambda h: lambda v: v.tensor_scalar(
            rows[0:1, h * 8 * C:h * 8 * C + C], onesf[0:1, 0:C], de[0:1, h:h + 1], de[0:1, h:h + 1],
            ALU.mult, ALU.mult))(h), deps=[o_sig, o_ones])
        head_consts_done[h] += [e2, e3]

    setup_done = [o_st0, o_stat0, o_nsq0, o_id] + [x for hh in head_consts_done for x in hh]

    last_krw_use = [None, None]
    last_vrw_use = [None, None]
    last_rfm_use = [None, None]
    last_wkv_store = [None, None]
    prev_tail = setup_done
    _pt29 = [None]

    for c in range(nchunk):
        par = c % 2
        ldeps = [d for d in ([last_krw_use[par], last_vrw_use[par], last_rfm_use[par]] + (prev_tail if c == 0 else [])) if d is not None]
        o_lk = P.add("g", (lambda c, par: lambda g: g.dma_start(
            out=krw[:, par * H64:(par + 1) * H64], in_=krows[c * C:(c + 1) * C, :]))(c, par),
            deps=[x for x in [last_krw_use[par]] if x is not None], lane=par, inc=16)
        o_lv = P.add("g", (lambda c, par: lambda g: g.dma_start(
            out=vrw[:, par * H64:(par + 1) * H64], in_=vrows[c * C:(c + 1) * C, :]))(c, par),
            deps=[x for x in [last_vrw_use[par]] if x is not None], lane=par, inc=16)
        o_lrs = []
        for h_ in range(NH):
            o_lrs.append(P.add("g", (lambda c, par, h_: lambda g: g.dma_start(
                out=rfm_s[:, par * NH * C + h_ * C: par * NH * C + (h_ + 1) * C],
                in_=rfm[h_ * 64:(h_ + 1) * 64, c * C:(c + 1) * C]))(c, par, h_),
                deps=[x for x in [last_rfm_use[par]] if x is not None], lane=par, inc=16))
        o_lr = o_lrs[-1]

        kuse, vuse, ruse = [], [], []
        for h in range(NH):
            prev_t29 = globals().get('_PT29') if False else _pt29[0]
            spar = (c * NH + h) % 2
            kslc = krw[:, par * H64 + h * 64: par * H64 + (h + 1) * 64]
            vslc = vrw[:, par * H64 + h * 64: par * H64 + (h + 1) * 64]
            rslc = rfm_s[:, par * NH * C + h * C: par * NH * C + (h + 1) * C]
            stslc = ST[:, h * 64:(h + 1) * 64]
            M = h * 128
            Pc = h * 8
            R = h * 8 * C
            wslc = wkv_s[:, spar * C:(spar + 1) * C]
            hc = head_consts_done[h]
            dep0 = [o_lrs[-1]] + prev_tail

            t1 = P.add("t", (lambda kslc: lambda t: t.transpose(psA[0:64, 0:128], kslc, ident[:, :]))(kslc),
                       deps=dep0 + [o_id])
            t2 = P.add("v", lambda v: v.tensor_copy(kfm[:, :], psA[0:64, 0:128]), deps=[t1])
            t3 = P.add("t", lambda t: t.matmul(psB[0:128, 0:128], kfm[:, :], kfm[:, :],
                                               start=True, stop=True), deps=[t2])
            t4 = P.add("v", lambda v: v.tensor_copy(Gs[:, :], psB[0:128, 0:128]), deps=[t3])
            t5 = P.add("v", (lambda M: lambda v: v.tensor_tensor(Lm[:, :], Gs[:, :], md[:, M:M + 128], ALU.mult))(M),
                       deps=[t4] + hc[0:1])
            t6 = P.add("v", (lambda M: lambda v: v.tensor_tensor(LTm[:, :], Gs[:, :], mdT[:, M:M + 128], ALU.mult))(M),
                       deps=[t4] + hc[1:2])
            t7 = P.add("t", (lambda stslc: lambda t: t.matmul(psA[0:128, 0:64], kfm[:, :], stslc,
                                                              start=True, stop=True))(stslc), deps=[t2, t3])
            t8a = P.add("v", (lambda Pc: lambda v: v.tensor_scalar(X2[:, 0:64], psA[0:64, 0:64],
                                                                   pcol[0:64, Pc:Pc + 1], None, ALU.mult))(Pc),
                        deps=[t7] + hc[3:4])
            t8 = P.add("v", (lambda Pc: lambda v: v.tensor_scalar(X2[:, 64:128], psA[64:128, 0:64],
                                                                  pcol[64:128, Pc:Pc + 1], None, ALU.mult))(Pc),
                       deps=[t7, t8a] + hc[3:4])
            t9 = P.add("t", (lambda vslc: lambda t: t.matmul(psB[0:128, 0:64], LTm[:, :], vslc,
                                                             start=True, stop=True))(vslc), deps=[t6, t4])
            t10a = P.add("v", lambda v: v.tensor_tensor(X2[:, 0:64], X2[:, 0:64], psB[0:64, 0:64], ALU.add),
                         deps=[t9, t8])
            t10 = P.add("v", lambda v: v.tensor_tensor(X2[:, 64:128], X2[:, 64:128], psB[64:128, 0:64], ALU.add),
                        deps=[t9, t8, t10a])
            pe_ctr = [0]
            last = t10
            lastL = [t5, t6]
            for blk in range(2):
                r0 = blk * 64
                lcopy = P.add("v", (lambda r0: lambda v: v.tensor_copy(
                    pwT[:, 0:64], LTm[r0:r0 + 64, r0:r0 + 64]))(r0), deps=lastL + [t10, t8])
                lcopyP = P.add("v", (lambda r0: lambda v: v.tensor_copy(
                    pwP[:, 0:64], Lm[r0:r0 + 64, r0:r0 + 64]))(r0), deps=lastL + [t10, t8])
                prevP, prevT = lcopyP, lcopy
                for lev in range(1, 6):
                    m1 = P.add("t", (lambda lev: lambda t: t.matmul(
                        psA[0:64, 0:64], pwP[:, 0:64], pwT[:, (lev - 1) * 64:lev * 64],
                        start=True, stop=True))(lev), deps=[prevP, prevT])
                    e1_ = P.add("v", (lambda lev: lambda v: v.tensor_copy(
                        pwT[:, lev * 64:(lev + 1) * 64], psA[0:64, 0:64]))(lev), deps=[m1])
                    if lev < 5:
                        m2 = P.add("t", (lambda lev: lambda t: t.matmul(
                            psB[0:64, 0:64], pwT[:, (lev - 1) * 64:lev * 64], pwP[:, 0:64],
                            start=True, stop=True))(lev), deps=[prevP, prevT, e1_])
                        e2_ = P.add("v", lambda v: v.tensor_copy(pwP[:, 64:128], psB[0:64, 0:64]), deps=[m2])
                        e3_ = P.add("v", lambda v: v.tensor_copy(pwP[:, 0:64], pwP[:, 64:128]), deps=[e2_])
                        prevP, prevT = e3_, e1_
                    else:
                        prevT = e1_
                if blk == 1:
                    mo = P.add("t", lambda t: t.matmul(psA[0:64, 0:64], LTm[0:64, 64:128], X2[:, 0:64],
                                                       start=True, stop=True), deps=[last, prevT] + lastL)
                    last = P.add("v", lambda v: v.tensor_tensor(X2[:, 64:128], X2[:, 64:128],
                                                                psA[0:64, 0:64], ALU.subtract), deps=[mo])
                for lev in [5, 4, 3, 2, 1, 0]:
                    op = ALU.subtract if lev == 0 else ALU.add
                    ma = P.add("t", (lambda lev, r0: lambda t: t.matmul(
                        psB[0:64, 0:64], pwT[:, lev * 64:(lev + 1) * 64], X2[:, r0:r0 + 64],
                        start=True, stop=True))(lev, r0), deps=[last, prevT])
                    last = P.add("v", (lambda r0, op: lambda v: v.tensor_tensor(
                        X2[:, r0:r0 + 64], X2[:, r0:r0 + 64], psB[0:64, 0:64], op))(r0, op), deps=[ma])
            tx1 = P.add("v", lambda v: v.tensor_copy(Xs[0:64, :], X2[:, 0:64]), deps=[last])
            tx2 = P.add("v", lambda v: v.tensor_copy(Xs[64:128, :], X2[:, 64:128]), deps=[last, tx1])
            t20 = P.add("v", (lambda vslc: lambda v: v.tensor_tensor(Es[:, :], Xs[:, :], vslc, ALU.subtract))(vslc),
                        deps=[tx2])
            t21 = P.add("v", (lambda kslc, Pc: lambda v: v.tensor_scalar(
                ksc[:, :], kslc, pcol[:, Pc + 1:Pc + 2], None, ALU.mult))(kslc, Pc), deps=dep0 + hc[4:5])
            t24 = P.add("t", (lambda stslc, rslc: lambda t: t.matmul(
                psC[0:64, 0:128], stslc, rslc, start=True, stop=True))(stslc, rslc),
                deps=dep0 + [t7] + ([prev_t29] if prev_t29 is not None else []))
            t22 = P.add("t", lambda t: t.matmul(psA[0:64, 0:64], ksc[:, :], Es[:, :],
                                                start=True, stop=True), deps=[t20, t21, t36_prev] if False else [t20, t21])
            t23 = P.add("v", (lambda stslc, Pc: lambda v: v.scalar_tensor_tensor(
                stslc, stslc, pcol[0:64, Pc + 4:Pc + 5], psA[0:64, 0:64], ALU.mult, ALU.subtract))(stslc, Pc),
                deps=[t22, t7, t24] + hc[7:8])
            t25 = P.add("t", (lambda rslc: lambda t: t.matmul(psB[0:128, 0:128], kfm[:, :], rslc,
                                                              start=True, stop=True))(rslc), deps=[t2] + [last])
            t26 = P.add("v", (lambda M: lambda v: v.tensor_tensor(GrT[:, :], psB[0:128, 0:128],
                                                                  mdrT[:, M:M + 128], ALU.mult))(M),
                        deps=[t25] + hc[2:3])
            t27 = P.add("v", (lambda h: lambda v: v.tensor_tensor(qr_s[:, :], psC[0:64, 0:128],
                                                                  powB[0:64, h * C:(h + 1) * C], ALU.mult))(h),
                        deps=[t24] + hc[8:9])
            t28 = P.add("t", lambda t: t.matmul(psC[0:64, 0:128], Es[:, :], GrT[:, :],
                                                start=True, stop=True), deps=[t26, t20, t27])
            t29 = P.add("v", (lambda wslc: lambda v: v.tensor_tensor(wslc, qr_s[:, :],
                                                                     psC[0:64, 0:128], ALU.subtract))(wslc),
                        deps=[t28] + [x for x in [last_wkv_store[spar]] if x is not None])
            t30 = P.add("v", lambda v: v.tensor_tensor(ksc[:, :], Es[:, :], Xs[:, :], ALU.mult),
                        deps=[t20, t22])
            t31 = P.add("v", lambda v: v.reduce_sum(sc1[:, 0:1], ksc[:, :], axis=AX.X), deps=[t30])
            t32 = P.add("v", lambda v: v.tensor_tensor(ksc[:, :], Es[:, :], Es[:, :], ALU.mult),
                        deps=[t31])
            t33 = P.add("v", lambda v: v.reduce_sum(sc1[:, 1:2], ksc[:, :], axis=AX.X), deps=[t32])
            t34 = P.add("v", (lambda Pc: lambda v: v.tensor_scalar(sc1[:, 2:3], sc1[:, 0:1],
                                                                   pcol[:, Pc + 2:Pc + 3], None, ALU.mult))(Pc),
                        deps=[t31] + hc[5:6])
            t35 = P.add("v", (lambda Pc: lambda v: v.scalar_tensor_tensor(
                phic[:, 0:1], sc1[:, 1:2], pcol[:, Pc + 3:Pc + 4], sc1[:, 2:3], ALU.mult, ALU.add))(Pc),
                deps=[t33, t34] + hc[6:7])
            t36 = P.add("t", lambda t: t.transpose(psA[0:1, 0:128], phic[:, 0:1], ident[:, :]), deps=[t35, t23])
            t37 = P.add("v", (lambda R: lambda v: v.tensor_copy(rows[0:1, R + C:R + 2 * C], psA[0:1, 0:128]))(R),
                        deps=[t36])
            t38 = P.add("v", (lambda R, h: lambda v: v.tensor_tensor_scan(
                rows[0:1, R + 2 * C:R + 3 * C], rows[0:1, R:R + C], rows[0:1, R + C:R + 2 * C],
                nsqc[0:1, 8 * h:8 * h + 1], ALU.mult, ALU.add))(R, h), deps=[t37] + hc[9:11])
            t39 = P.add("s", (lambda R: lambda s: s.activation(rows[0:1, R + 3 * C:R + 4 * C],
                                                               rows[0:1, R + 2 * C:R + 3 * C], AF.Sqrt))(R),
                        deps=[t38])
            t40 = P.add("v", (lambda R: lambda v: v.tensor_scalar(rows[0:1, R + 3 * C:R + 4 * C],
                                                                  rows[0:1, R + 3 * C:R + 4 * C],
                                                                  1e-8, None, ALU.max))(R), deps=[t39])
            t41 = P.add("v", (lambda R: lambda v: v.reciprocal(rows[0:1, R + 6 * C:R + 7 * C],
                                                               rows[0:1, R + 3 * C:R + 4 * C]))(R), deps=[t40])
            t42 = P.add("v", (lambda R: lambda v: v.tensor_scalar(rows[0:1, R + 3 * C:R + 4 * C],
                                                                  rows[0:1, R + 6 * C:R + 7 * C],
                                                                  16.0, 1.0, ALU.mult, ALU.min))(R), deps=[t41])
            t43 = P.add("v", (lambda R: lambda v: v.tensor_tensor_scan(
                rows[0:1, R + 4 * C:R + 5 * C], rows[0:1, R + 3 * C:R + 4 * C],
                rows[0:1, R + 5 * C:R + 6 * C], 1.0, ALU.mult, ALU.mult))(R), deps=[t42] + hc[9:10])
            t44 = P.add("v", (lambda R, h: lambda v: v.tensor_tensor(
                nsqc[0:1, 8 * h + 1:8 * h + 2], rows[0:1, R + 5 * C - 1:R + 5 * C],
                rows[0:1, R + 5 * C - 1:R + 5 * C], ALU.mult))(R, h), deps=[t43])
            t45 = P.add("v", (lambda R, h: lambda v: v.tensor_tensor(
                nsqc[0:1, 8 * h:8 * h + 1], rows[0:1, R + 3 * C - 1:R + 3 * C],
                nsqc[0:1, 8 * h + 1:8 * h + 2], ALU.mult))(R, h), deps=[t44, t38])
            t46 = P.add("t", (lambda R: lambda t: t.matmul(psA[0:64, 0:128], onesf[0:1, 0:64],
                                                           rows[0:1, R + 4 * C:R + 5 * C],
                                                           start=True, stop=True))(R), deps=[t43, t36])
            t47 = P.add("v", (lambda wslc: lambda v: v.tensor_tensor(wslc, wslc, psA[0:64, 0:128],
                                                                     ALU.mult))(wslc), deps=[t46, t29])
            t48 = P.add("t", (lambda R: lambda t: t.matmul(psB[0:64, 0:1], onesf[0:1, 0:64],
                                                           rows[0:1, R + 5 * C - 1:R + 5 * C],
                                                           start=True, stop=True))(R), deps=[t43, t28])
            t49 = P.add("v", lambda v: v.tensor_copy(ppc[:, :], psB[0:64, 0:1]), deps=[t48])
            t50 = P.add("v", (lambda stslc: lambda v: v.tensor_scalar(stslc, stslc, ppc[0:64, 0:1],
                                                                      None, ALU.mult))(stslc), deps=[t49, t23])
            t51 = P.add("v", (lambda wslc: lambda v: v.reduce_sum(sc1[0:64, 3:4], wslc, axis=AX.X))(wslc),
                        deps=[t47])
            t52 = P.add("v", (lambda h: lambda v: v.tensor_tensor(stat[:, 2 * h:2 * h + 1],
                                                                  stat[:, 2 * h:2 * h + 1],
                                                                  sc1[0:64, 3:4], ALU.add))(h),
                        deps=[t51, o_stat0])
            t53 = P.add("v", (lambda wslc: lambda v: v.tensor_tensor(sq_s[:, :], wslc, wslc, ALU.mult))(wslc),
                        deps=[t47])
            t54 = P.add("v", lambda v: v.reduce_sum(sc1[0:64, 3:4], sq_s[:, :], axis=AX.X), deps=[t53, t52])
            t55 = P.add("v", (lambda h: lambda v: v.tensor_tensor(stat[:, 2 * h + 1:2 * h + 2],
                                                                  stat[:, 2 * h + 1:2 * h + 2],
                                                                  sc1[0:64, 3:4], ALU.add))(h), deps=[t54])
            t56 = P.add("g", (lambda wslc, h, c: lambda g: g.dma_start(
                out=wkvfm[h * 64:(h + 1) * 64, c * C:(c + 1) * C], in_=wslc))(wslc, h, c),
                deps=[t47], lane=2 + spar, inc=16)
            last_wkv_store[spar] = t56
            kuse.append(t21)
            vuse.append(t20)
            ruse.append(t25)
            prev_tail = [t50, t55, t45, t47]
            _pt29[0] = t29
        last_krw_use[par] = kuse[-1]
        last_vrw_use[par] = vuse[-1]
        last_rfm_use[par] = ruse[-1]

    fin = []
    for h in range(NH):
        f1 = P.add("t", (lambda h: lambda t: t.matmul(psA[0:1, 0:2], onesf[0:64, 0:1],
                                                      stat[0:64, 2 * h:2 * h + 2],
                                                      start=True, stop=True))(h),
                   deps=prev_tail + ([] if h == 0 else [fin[-1]]))
        f2 = P.add("v", (lambda h: lambda v: v.tensor_copy(stato[0:1, 2 * h:2 * h + 2], psA[0:1, 0:2]))(h),
                   deps=[f1])
        fin.append(f2)
    o_st = P.add("g", lambda g: g.dma_start(out=gnstats[:, :], in_=stato[0:1, :]), deps=fin, lane=0, inc=16)
    fin_ops = [o_st] + [last_wkv_store[0], last_wkv_store[1]]
    val_after = P.emit(wait=wait)
    return P, [f for f in fin_ops if f is not None]


BF = ml_dtypes.bfloat16


def _psts(nc, n=8):
    return [nc.alloc_psum_tensor(f"ps{i}", [128, 512], F32) for i in range(n)]


def build_ln(Tt=TOK):
    nc = bass.Bass("TRN2", target_bir_lowering=False)
    xi = nc.dram_tensor("x", [Tt, D], F32, kind="ExternalInput").ap()
    xo = nc.dram_tensor("xn", [Tt, D], BF16, kind="ExternalOutput").ap()
    with nc.Block() as block:
        ctx = Ctx(nc, block)
        outp = SemPair(ctx, "out")
        emit_ln(ctx, "ln0", xi, xo, Tt, outp)
        emit_finish_all(ctx, [(outp.s[0], outp.s[0].n), (outp.s[1], outp.s[1].n)])
    return nc


def build_mix0():
    nc = bass.Bass("TRN2", target_bir_lowering=False)
    xnb = nc.dram_tensor("xnb", [S, D], BF16, kind="ExternalInput").ap()
    tmr = nc.dram_tensor("tmr", [1, D], F32, kind="ExternalInput").ap()
    tmk = nc.dram_tensor("tmk", [1, D], F32, kind="ExternalInput").ap()
    tmv = nc.dram_tensor("tmv", [1, D], F32, kind="ExternalInput").ap()
    xr = nc.dram_tensor("xr", [S, D], BF16, kind="ExternalOutput").ap()
    xk = nc.dram_tensor("xk", [S, D], BF16, kind="ExternalOutput").ap()
    xv = nc.dram_tensor("xv", [S, D], BF16, kind="ExternalOutput").ap()
    with (nc.psum_tensor("ps0", [128, 512], F32) as ps0,
          nc.psum_tensor("ps1", [128, 512], F32) as ps1,
          nc.Block() as block):
        ctx = Ctx(nc, block)
        fins = []
        for tag, tm, out in [("mr", tmr, xr), ("mk", tmk, xk), ("mv", tmv, xv)]:
            P, fs = emit_mix(ctx, tag, xnb, tm, out, S, [ps0, ps1])
            fins += plan_fin(P, fs)
        emit_finish_all(ctx, fins)
    return nc


def build_l0head():
    nc = bass.Bass("TRN2", target_bir_lowering=False)
    xr = nc.dram_tensor("xr", [S, D], BF16, kind="ExternalInput").ap()
    xk = nc.dram_tensor("xk", [S, D], BF16, kind="ExternalInput").ap()
    xv = nc.dram_tensor("xv", [S, D], BF16, kind="ExternalInput").ap()
    xnf = nc.dram_tensor("xnf", [2 * S, D], BF16, kind="ExternalInput").ap()
    wr = nc.dram_tensor("wr", [D, 256], F32, kind="ExternalInput").ap()
    wk = nc.dram_tensor("wk", [D, 256], F32, kind="ExternalInput").ap()
    wv = nc.dram_tensor("wv", [D, 256], F32, kind="ExternalInput").ap()
    br = nc.dram_tensor("br", [256], F32, kind="ExternalInput").ap()
    bk = nc.dram_tensor("bk", [256], F32, kind="ExternalInput").ap()
    bv = nc.dram_tensor("bv", [256], F32, kind="ExternalInput").ap()
    phiT = nc.dram_tensor("phiT", [D, 256], F32, kind="ExternalInput").ap()
    phib = nc.dram_tensor("phib", [256], F32, kind="ExternalInput").ap()
    ew = nc.dram_tensor("ew", [8 * 1024, 1024], F32, kind="ExternalInput").ap()
    eb = nc.dram_tensor("eb", [1, 8192], F32, kind="ExternalInput").ap()
    rfm = nc.dram_tensor("rfm", [256, S], F32, kind="ExternalOutput").ap()
    krows = nc.dram_tensor("krows", [S, 256], F32, kind="ExternalOutput").ap()
    vrows = nc.dram_tensor("vrows", [S, 256], F32, kind="ExternalOutput").ap()
    ytil = nc.dram_tensor("ytil", [256, 1024], BF16, kind="ExternalOutput").ap()
    kpre = nc.dram_tensor("kpre", [S, 256], F32, kind="Internal").ap()
    vpre = nc.dram_tensor("vpre", [S, 256], F32, kind="Internal").ap()
    lg0 = nc.dram_tensor("lg0", [2 * S, 256], F32, kind="Internal").ap()
    ddr = nc.dram_tensor("ddr", [2 * S, 256], BF16, kind="Internal").ap()
    pst = _psts(nc)
    with nc.Block() as block:
        ctx = Ctx(nc, block)
        fins = []
        pr = Proj(ctx, "pr", S, D, 256, [xr], [wr], rfm, "fm", bias_dram=br,
                  act=AF.Sigmoid, ps=pst[0:2])
        op1, _ = pr.emit()
        fins += proj_fin(op1)
        pk = Proj(ctx, "pk", S, D, 256, [xk], [wk], kpre, "tok", bias_dram=bk, ps=pst[2:4])
        opk, _ = pk.emit()
        pv = Proj(ctx, "pv", S, D, 256, [xv], [wv], vpre, "tok", bias_dram=bv, ps=pst[4:6])
        opv, _ = pv.emit()
        P2, f2 = emit_l2norm(ctx, "l2k", kpre, krows, S, 4, wait=proj_fin(opk))
        fins += plan_fin(P2, f2)
        P3, f3 = emit_l2norm(ctx, "l2v", vpre, vrows, S, 4, wait=proj_fin(opv))
        fins += plan_fin(P3, f3)
        pl = Proj(ctx, "pl", 2 * S, D, 256, [xnf], [phiT], lg0, "tok", bias_dram=phib,
                  ps=pst[6:8])
        opl, _ = pl.emit()
        P4, f4 = emit_ffn0_expert(ctx, "fx", xnf, lg0, ew, eb, ddr, ytil,
                                  psacc=pst[0:4], psrot=pst[4:6],
                                  wait=proj_fin(opl) + proj_fin(op1))
        fins += plan_fin(P4, f4)
        emit_finish_all(ctx, fins)
    return nc


def build_scan():
    nc = bass.Bass("TRN2", target_bir_lowering=False)
    rfm = nc.dram_tensor("rfm", [256, S], F32, kind="ExternalInput").ap()
    krows = nc.dram_tensor("krows", [S, 256], F32, kind="ExternalInput").ap()
    vrows = nc.dram_tensor("vrows", [S, 256], F32, kind="ExternalInput").ap()
    deceta = nc.dram_tensor("deceta", [1, 8], F32, kind="ExternalInput").ap()
    wkvfm = nc.dram_tensor("wkvfm", [256, S], F32, kind="ExternalOutput").ap()
    gns = nc.dram_tensor("gns", [1, 8], F32, kind="ExternalOutput").ap()
    psA = nc.alloc_psum_tensor("psA", [128, 512], F32)
    psB = nc.alloc_psum_tensor("psB", [128, 512], F32)
    psC = nc.alloc_psum_tensor("psC", [128, 512], F32)
    with nc.Block() as block:
        ctx = Ctx(nc, block)
        Pl, fins = emit_scan(ctx, "sc", rfm, krows, vrows, deceta, wkvfm, gns,
                             psA, psB, psC, NH=4, C=128, SEQ=S)
        emit_finish_all(ctx, plan_fin(Pl, fins))
    return nc


def build_l0tok():
    nc = bass.Bass("TRN2", target_bir_lowering=False)
    xin = nc.dram_tensor("x", [TOK, D], F32, kind="ExternalInput").ap()
    xn = nc.dram_tensor("xn", [TOK, D], BF16, kind="ExternalInput").ap()
    xnprev = nc.dram_tensor("xnprev", [1, D], BF16, kind="ExternalInput").ap()
    wkvm = nc.dram_tensor("wkvm", [D, TOK], F32, kind="ExternalInput").ap()
    gnsb = nc.dram_tensor("gnsb", [16, 2], F32, kind="ExternalInput").ap()
    gnw = nc.dram_tensor("gnw", [D], F32, kind="ExternalInput").ap()
    gnb = nc.dram_tensor("gnb", [D], F32, kind="ExternalInput").ap()
    PHc = nc.dram_tensor("PH", [16, 128], F32, kind="ExternalInput").ap()
    CHMc = nc.dram_tensor("CHM", [16, 8], F32, kind="ExternalInput").ap()
    woT = nc.dram_tensor("woT", [D, D], F32, kind="ExternalInput").ap()
    bo = nc.dram_tensor("bo", [D], F32, kind="ExternalInput").ap()
    wgT = nc.dram_tensor("wgT", [D, D], F32, kind="ExternalInput").ap()
    bg = nc.dram_tensor("bg", [D], F32, kind="ExternalInput").ap()
    tmg = nc.dram_tensor("tmg", [1, D], F32, kind="ExternalInput").ap()
    phiT = nc.dram_tensor("phiT", [D, 2048], F32, kind="ExternalInput").ap()
    phib = nc.dram_tensor("phib", [2048], F32, kind="ExternalInput").ap()
    ytil = nc.dram_tensor("ytil", [2048, D], BF16, kind="ExternalInput").ap()
    e32 = nc.dram_tensor("e32", [128, 32], F32, kind="ExternalInput").ap()
    e32T = nc.dram_tensor("e32T", [32, 128], F32, kind="ExternalInput").ap()
    x1 = nc.dram_tensor("x1", [TOK, D], F32, kind="ExternalOutput").ap()
    xn1 = nc.dram_tensor("xn1", [TOK, D], BF16, kind="ExternalOutput").ap()
    wkvn = nc.dram_tensor("wkvn", [D, TOK], BF16, kind="Internal").ap()
    s0d = nc.dram_tensor("s0d", [TOK, D], F32, kind="Internal").ap()
    xg = nc.dram_tensor("xg", [TOK, D], BF16, kind="Internal").ap()
    g0d = nc.dram_tensor("g0d", [TOK, D], BF16, kind="Internal").ap()
    lgfm = nc.dram_tensor("lgfm", [2048, TOK], F32, kind="Internal").ap()
    gfd = nc.dram_tensor("gfd", [TOK, D], BF16, kind="Internal").ap()
    pst = _psts(nc)
    with nc.Block() as block:
        ctx = Ctx(nc, block)
        fins = []
        Pg, fg = emit_gnapply(ctx, "gn", wkvm, gnsb, gnw, gnb, PHc, CHMc, wkvn, pst[0:1],
                              T=TOK, SEQ=S)
        SH = ("sh_x", "sh_w", 8 * 2048, 16 * 512)
        ps0 = Proj(ctx, "s0", TOK, D, D, [wkvn], [woT], s0d, "tok", bias_dram=bo,
                   ps=pst[2:4], x_fm=True, share=SH)
        op0, _ = ps0.emit(wait=plan_fin(Pg, fg))
        fins += proj_fin(op0)
        Pm, fm = emit_mix(ctx, "mg", xn, tmg, xg, TOK, pst[4:6], prev_dram=xnprev)
        pg0 = Proj(ctx, "g0", TOK, D, D, [xg], [wgT], g0d, "tok", bias_dram=bg,
                   act=AF.Silu, ps=pst[6:8], share=SH)
        opg, _ = pg0.emit(wait=plan_fin(Pm, fm) + proj_fin(op0))
        plg = Proj(ctx, "lgf", TOK, D, 2048, [xn], [phiT], lgfm, "fm", bias_dram=phib,
                   ps=pst[0:2], share=SH)
        oplg, _ = plg.emit(wait=plan_fin(Pg, fg) + proj_fin(opg))
        Pf, ff = emit_ffn0_tok(ctx, "ft", lgfm, ytil, e32, e32T, gfd, psacc=pst[2:4],
                               psrot=pst[4:6], T=TOK,
                               wait=proj_fin(oplg) + proj_fin(op0) + plan_fin(Pm, fm))
        Pc, fc = emit_combine(ctx, "cb", xin, s0d, g0d, gfd, x1, T=TOK,
                              wait=proj_fin(op0) + proj_fin(opg) + plan_fin(Pf, ff))
        outp = SemPair(ctx, "lnout")
        fincb = plan_fin(Pc, fc)
        def gate(gg):
            for sem, val in fincb:
                gg.wait_ge(sem.h, val)
        ctx.y(gate)
        emit_ln(ctx, "ln1", x1, xn1, TOK, outp)
        fins += fincb + [(outp.s[0], outp.s[0].n), (outp.s[1], outp.s[1].n)]
        emit_finish_all(ctx, fins)
    return nc


def build_qzkv():
    nc = bass.Bass("TRN2", target_bir_lowering=False)
    xn = nc.dram_tensor("xn", [TOK, D], BF16, kind="ExternalInput").ap()
    wq = nc.dram_tensor("wq", [D, D], F32, kind="ExternalInput").ap()
    wk = nc.dram_tensor("wk", [D, 192], F32, kind="ExternalInput").ap()
    wv = nc.dram_tensor("wv", [D, 192], F32, kind="ExternalInput").ap()
    q = nc.dram_tensor("q", [TOK, D], BF16, kind="ExternalOutput").ap()
    zk = nc.dram_tensor("zk", [TOK, 256], BF16, kind="ExternalOutput").ap()
    zv = nc.dram_tensor("zv", [TOK, 256], BF16, kind="ExternalOutput").ap()
    pst = _psts(nc, 6)
    with nc.Block() as block:
        ctx = Ctx(nc, block)
        p1 = Proj(ctx, "pq", TOK, D, D, [xn], [wq], q, "tok", ps=pst[0:2])
        o1, _ = p1.emit()
        p2 = Proj(ctx, "pk", TOK, D, 192, [xn], [wk], zk[:, 0:192], "tok", ps=pst[2:4])
        o2, _ = p2.emit()
        p3 = Proj(ctx, "pv", TOK, D, 192, [xn], [wv], zv[:, 0:192], "tok", ps=pst[4:6])
        o3, _ = p3.emit()
        emit_finish_all(ctx, proj_fin(o1) + proj_fin(o2) + proj_fin(o3))
    return nc


def build_l1attn():
    nc = bass.Bass("TRN2", target_bir_lowering=False)
    qf = nc.dram_tensor("qf", [S, 256], BF16, kind="ExternalInput").ap()
    zkf = nc.dram_tensor("zkf", [S, 256], BF16, kind="ExternalInput").ap()
    zvf = nc.dram_tensor("zvf", [S, 256], BF16, kind="ExternalInput").ap()
    mp = nc.dram_tensor("mp", [192, 256], F32, kind="ExternalInput").ap()
    msg = nc.dram_tensor("msg", [192, 256], F32, kind="ExternalInput").ap()
    vdc = nc.dram_tensor("vdc", [192, 256], F32, kind="ExternalInput").ap()
    cr = nc.dram_tensor("cr", [64, S], F32, kind="ExternalInput").ap()
    sr = nc.dram_tensor("sr", [64, S], F32, kind="ExternalInput").ap()
    yfm = nc.dram_tensor("yfm", [256, S], BF16, kind="ExternalOutput").ap()
    pst = _psts(nc, 7)
    with nc.Block() as block:
        ctx = Ctx(nc, block)
        P, fs = emit_l1attn(ctx, "fa", qf, zkf, zvf, mp, msg, vdc, cr, sr, yfm, pst)
        emit_finish_all(ctx, plan_fin(P, fs))
    return nc


def build_l1tok():
    nc = bass.Bass("TRN2", target_bir_lowering=False)
    xin = nc.dram_tensor("x", [TOK, D], F32, kind="ExternalInput").ap()
    xn = nc.dram_tensor("xn", [TOK, D], BF16, kind="ExternalInput").ap()
    yfm = nc.dram_tensor("yfm", [D, TOK], BF16, kind="ExternalInput").ap()
    wproj = nc.dram_tensor("wproj", [D, D], F32, kind="ExternalInput").ap()
    rwT = nc.dram_tensor("rwT", [D, 8], F32, kind="ExternalInput").ap()
    wg = nc.dram_tensor("wg", [8 * D, 2048], F32, kind="ExternalInput").ap()
    wu = nc.dram_tensor("wu", [8 * D, 2048], F32, kind="ExternalInput").ap()
    wd = nc.dram_tensor("wd", [8 * 2048, D], F32, kind="ExternalInput").ap()
    x2 = nc.dram_tensor("x2", [TOK, D], F32, kind="ExternalOutput").ap()
    xn2 = nc.dram_tensor("xn2", [TOK, D], BF16, kind="ExternalOutput").ap()
    s1d = nc.dram_tensor("s1d", [TOK, D], F32, kind="Internal").ap()
    rld = nc.dram_tensor("rld", [TOK, 8], F32, kind="Internal").ap()
    dwd = nc.dram_tensor("dwd", [TOK, 8], F32, kind="Internal").ap()
    f1o = nc.dram_tensor("f1o", [TOK, D], F32, kind="Internal").ap()
    pst = _psts(nc)
    with nc.Block() as block:
        ctx = Ctx(nc, block)
        SH = ("shx", "shw", 8 * 1024, 16 * 512)
        p1 = Proj(ctx, "s1", TOK, D, D, [yfm], [wproj], s1d, "tok", ps=pst[0:2],
                  x_fm=True, share=SH)
        o1, _ = p1.emit()
        p2 = Proj(ctx, "rl", TOK, D, 8, [xn], [rwT], rld, "tok", ps=pst[2:4], share=SH)
        o2, _ = p2.emit(wait=proj_fin(o1))
        Pr, fr_ = emit_route1(ctx, "rt", rld, dwd, T=TOK, wait=proj_fin(o2))
        Pm, fm_ = emit_moe(ctx, "moe", xn, wg, wu, wd, dwd, f1o, 8, 2048, True,
                           pst[4], pst[5], pst[6], pst[7], T=TOK, wait=plan_fin(Pr, fr_))
        Pc, fc_ = emit_combine2(ctx, "cb", xin, s1d, f1o, x2, T=TOK,
                                wait=proj_fin(o1) + plan_fin(Pm, fm_))
        outp = SemPair(ctx, "lnout")
        fincb = plan_fin(Pc, fc_)
        def gate(gg):
            for sem, val in fincb:
                gg.wait_ge(sem.h, val)
        ctx.y(gate)
        emit_ln(ctx, "ln2", x2, xn2, TOK, outp)
        emit_finish_all(ctx, fincb + [(outp.s[0], outp.s[0].n), (outp.s[1], outp.s[1].n)])
    return nc


def build_l2pre():
    nc = bass.Bass("TRN2", target_bir_lowering=False)
    xn = nc.dram_tensor("xn", [TOK, D], BF16, kind="ExternalInput").ap()
    wk = nc.dram_tensor("wk", [D, D], F32, kind="ExternalInput").ap()
    wv = nc.dram_tensor("wv", [D, D], F32, kind="ExternalInput").ap()
    wq = nc.dram_tensor("wq", [D, D], F32, kind="ExternalInput").ap()
    wdn = nc.dram_tensor("wdn", [D, 256], F32, kind="ExternalInput").ap()
    wuk = nc.dram_tensor("wuk", [256, D], F32, kind="ExternalInput").ap()
    wuv = nc.dram_tensor("wuv", [256, D], F32, kind="ExternalInput").ap()
    kvd = nc.dram_tensor("kvd", [TOK, D], BF16, kind="Internal").ap()
    latd = nc.dram_tensor("latd", [TOK, 256], BF16, kind="Internal").ap()
    krec = nc.dram_tensor("krec", [TOK, D], BF16, kind="ExternalOutput").ap()
    vrec = nc.dram_tensor("vrec", [TOK, D], BF16, kind="ExternalOutput").ap()
    q2 = nc.dram_tensor("q2", [D, TOK], BF16, kind="ExternalOutput").ap()
    pst = _psts(nc)
    with nc.Block() as block:
        ctx = Ctx(nc, block)
        pkv = Proj(ctx, "kv", TOK, D, D, [xn, xn], [wk, wv], kvd, "tok", ps=pst[0:2])
        okv, _ = pkv.emit()
        plat = Proj(ctx, "lat", TOK, D, 256, [kvd], [wdn], latd, "tok", ps=pst[2:4])
        olat, _ = plat.emit(wait=proj_fin(okv))
        pkr = Proj(ctx, "kr", TOK, 256, D, [latd], [wuk], krec, "tok", ps=pst[4:6])
        okr, _ = pkr.emit(wait=proj_fin(olat))
        pvr = Proj(ctx, "vr", TOK, 256, D, [latd], [wuv], vrec, "tok", ps=pst[6:8])
        ovr, _ = pvr.emit(wait=proj_fin(olat))
        pq = Proj(ctx, "q2", TOK, D, D, [xn], [wq], q2, "fm", ps=pst[0:2])
        oq, _ = pq.emit(wait=proj_fin(okv))
        emit_finish_all(ctx, proj_fin(okr) + proj_fin(ovr) + proj_fin(oq))
    return nc


def build_l2attn():
    nc = bass.Bass("TRN2", target_bir_lowering=False)
    qfm = nc.dram_tensor("qfm", [D, TOK], BF16, kind="ExternalInput").ap()
    kf = nc.dram_tensor("kf", [S, D], BF16, kind="ExternalInput").ap()
    vf = nc.dram_tensor("vf", [S, D], BF16, kind="ExternalInput").ap()
    uqp = nc.dram_tensor("uqp", [1, TOK], F32, kind="ExternalInput").ap()
    uqm = nc.dram_tensor("uqm", [1, TOK], F32, kind="ExternalInput").ap()
    kdpd = nc.dram_tensor("kdp", [1, S], F32, kind="ExternalInput").ap()
    kdmd = nc.dram_tensor("kdm", [1, S], F32, kind="ExternalInput").ap()
    ofm = nc.dram_tensor("ofm", [D, TOK], BF16, kind="ExternalOutput").ap()
    pst = _psts(nc)
    with nc.Block() as block:
        ctx = Ctx(nc, block)
        P, fs = emit_l2flash(ctx, "f2", qfm, kf, vf, uqp, uqm, kdpd, kdmd, ofm, pst)
        emit_finish_all(ctx, plan_fin(P, fs))
    return nc


def build_l2tok():
    nc = bass.Bass("TRN2", target_bir_lowering=False)
    xin = nc.dram_tensor("x", [TOK, D], F32, kind="ExternalInput").ap()
    xn = nc.dram_tensor("xn", [TOK, D], BF16, kind="ExternalInput").ap()
    ofm = nc.dram_tensor("ofm", [D, TOK], BF16, kind="ExternalInput").ap()
    wo = nc.dram_tensor("wo", [D, D], F32, kind="ExternalInput").ap()
    w1T = nc.dram_tensor("w1T", [D, 4096], F32, kind="ExternalInput").ap()
    w1s = nc.dram_tensor("w1s", [4096], F32, kind="ExternalInput").ap()
    w2T = nc.dram_tensor("w2T", [D, 4096], F32, kind="ExternalInput").ap()
    w2s = nc.dram_tensor("w2s", [4096], F32, kind="ExternalInput").ap()
    h1T = nc.dram_tensor("h1T", [4096, 4096], F32, kind="ExternalInput").ap()
    h1s = nc.dram_tensor("h1s", [4096], F32, kind="ExternalInput").ap()
    h2T = nc.dram_tensor("h2T", [4096, 4096], F32, kind="ExternalInput").ap()
    h2s = nc.dram_tensor("h2s", [4096], F32, kind="ExternalInput").ap()
    w3T = nc.dram_tensor("w3T", [4096, D], F32, kind="ExternalInput").ap()
    w3s = nc.dram_tensor("w3s", [D], F32, kind="ExternalInput").ap()
    x3 = nc.dram_tensor("x3", [TOK, D], F32, kind="ExternalOutput").ap()
    xn3 = nc.dram_tensor("xn3", [TOK, D], BF16, kind="ExternalOutput").ap()
    s2d = nc.dram_tensor("s2d", [TOK, D], F32, kind="Internal").ap()
    y1d = nc.dram_tensor("y1d", [TOK, 4096], F32, kind="Internal").ap()
    y2d = nc.dram_tensor("y2d", [TOK, 4096], F32, kind="Internal").ap()
    h0d = nc.dram_tensor("h0d", [TOK, 4096], BF16, kind="Internal").ap()
    h2d = nc.dram_tensor("h2d", [TOK, 4096], BF16, kind="Internal").ap()
    h3d = nc.dram_tensor("h3d", [TOK, 4096], BF16, kind="Internal").ap()
    f2o = nc.dram_tensor("f2o", [TOK, D], F32, kind="Internal").ap()
    pst = _psts(nc)
    with nc.Block() as block:
        ctx = Ctx(nc, block)
        SH = ("shx", "shw", 32 * 1024, 2 * 32 * 512)
        ps2 = Proj(ctx, "s2", TOK, D, D, [ofm], [wo], s2d, "tok", ps=pst[0:2],
                   x_fm=True, share=SH)
        o2w, _ = ps2.emit()
        prev = proj_fin(o2w)
        py1 = Proj(ctx, "y1", TOK, D, 4096, [xn], [w1T], y1d, "tok", ps=pst[2:4],
                   colscale_dram=w1s, share=SH)
        oy1, _ = py1.emit(wait=prev)
        py2 = Proj(ctx, "y2", TOK, D, 4096, [xn], [w2T], y2d, "tok", ps=pst[4:6],
                   colscale_dram=w2s, share=SH)
        oy2, _ = py2.emit(wait=proj_fin(oy1))
        Pg, fg = emit_gatemul(ctx, "gm", y1d, y2d, h0d, TOK, 4096,
                              wait=proj_fin(oy1) + proj_fin(oy2))
        prev = plan_fin(Pg, fg)
        for li, (hin, hw, hsc, hout) in enumerate([(h0d, h1T, h1s, h2d), (h2d, h2T, h2s, h3d)]):
            for nn in range(8):
                pp = Proj(ctx, f"hh{li}{nn}", TOK, 4096, 512, [hin],
                          [hw[:, nn * 512:(nn + 1) * 512]], hout[:, nn * 512:(nn + 1) * 512],
                          "tok", ps=pst[(2 * nn) % 6:(2 * nn) % 6 + 2],
                          colscale_dram=hsc[nn * 512:(nn + 1) * 512], act=AF.Gelu, share=SH)
                oo_, _ = pp.emit(wait=prev)
                prev = proj_fin(oo_)
        pw3 = Proj(ctx, "w3", TOK, 4096, D, [h3d], [w3T], f2o, "tok", ps=pst[6:8],
                   colscale_dram=w3s, share=SH)
        ow3, _ = pw3.emit(wait=prev)
        Pc, fc_ = emit_combine2(ctx, "cb", xin, s2d, f2o, x3, T=TOK,
                                wait=proj_fin(o2w) + proj_fin(ow3))
        outp = SemPair(ctx, "lnout")
        fincb = plan_fin(Pc, fc_)
        def gate(gg):
            for sem, val in fincb:
                gg.wait_ge(sem.h, val)
        ctx.y(gate)
        emit_ln(ctx, "ln3", x3, xn3, TOK, outp)
        emit_finish_all(ctx, fincb + [(outp.s[0], outp.s[0].n), (outp.s[1], outp.s[1].n)])
    return nc


def build_l3a():
    nc = bass.Bass("TRN2", target_bir_lowering=False)
    xn = nc.dram_tensor("xn", [TOK, D], BF16, kind="ExternalInput").ap()
    wq = nc.dram_tensor("wq", [D, D], F32, kind="ExternalInput").ap()
    wk = nc.dram_tensor("wk", [D, D], F32, kind="ExternalInput").ap()
    wv = nc.dram_tensor("wv", [D, D], F32, kind="ExternalInput").ap()
    wf = nc.dram_tensor("wf", [64, 64], F32, kind="ExternalInput").ap()
    bf = nc.dram_tensor("bf", [64], F32, kind="ExternalInput").ap()
    kvp = nc.dram_tensor("kvp", [D, 65], F32, kind="ExternalOutput").ap()
    qff = nc.dram_tensor("qff", [D, TOK], BF16, kind="ExternalOutput").ap()
    qfm = nc.dram_tensor("qfm", [D, TOK], BF16, kind="Internal").ap()
    kfm = nc.dram_tensor("kfm", [D, TOK], BF16, kind="Internal").ap()
    vtd = nc.dram_tensor("vtd", [TOK, D], BF16, kind="Internal").ap()
    pst = _psts(nc)
    with nc.Block() as block:
        ctx = Ctx(nc, block)
        SH = ("shx", "shw", 8 * 1024, 16 * 512)
        p1 = Proj(ctx, "q3", TOK, D, D, [xn], [wq], qfm, "fm", ps=pst[0:2], share=SH)
        o1, _ = p1.emit()
        p2 = Proj(ctx, "k3", TOK, D, D, [xn], [wk], kfm, "fm", ps=pst[2:4], share=SH)
        o2, _ = p2.emit(wait=proj_fin(o1))
        p3 = Proj(ctx, "v3", TOK, D, D, [xn], [wv], vtd, "tok", ps=pst[4:6], share=SH)
        o3, _ = p3.emit(wait=proj_fin(o2))
        Pa, fa = emit_l3a(ctx, "la", qfm, kfm, vtd, wf, bf, qff, kvp, pst,
                          wait=proj_fin(o1) + proj_fin(o2) + proj_fin(o3))
        emit_finish_all(ctx, plan_fin(Pa, fa))
    return nc


def build_l3b():
    nc = bass.Bass("TRN2", target_bir_lowering=False)
    xin = nc.dram_tensor("x", [TOK, D], F32, kind="ExternalInput").ap()
    xn = nc.dram_tensor("xn", [TOK, D], BF16, kind="ExternalInput").ap()
    kvps = nc.dram_tensor("kvps", [8 * D, 65], F32, kind="ExternalInput").ap()
    qff = nc.dram_tensor("qff", [D, TOK], BF16, kind="ExternalInput").ap()
    wo = nc.dram_tensor("wo", [D, D], F32, kind="ExternalInput").ap()
    bo = nc.dram_tensor("bo", [D], F32, kind="ExternalInput").ap()
    gwT = nc.dram_tensor("gwT", [D, 16], F32, kind="ExternalInput").ap()
    w1 = nc.dram_tensor("w1", [16 * D, 512], F32, kind="ExternalInput").ap()
    w2 = nc.dram_tensor("w2", [16 * 512, D], F32, kind="ExternalInput").ap()
    sf1 = nc.dram_tensor("sf1", [D, 2048], F32, kind="ExternalInput").ap()
    sf2 = nc.dram_tensor("sf2", [D, 2048], F32, kind="ExternalInput").ap()
    sf3 = nc.dram_tensor("sf3", [2048, D], F32, kind="ExternalInput").ap()
    yout = nc.dram_tensor("y", [TOK, D], F32, kind="ExternalOutput").ap()
    yfm = nc.dram_tensor("yfm", [D, TOK], BF16, kind="Internal").ap()
    s3d = nc.dram_tensor("s3d", [TOK, D], F32, kind="Internal").ap()
    a1d = nc.dram_tensor("a1d", [TOK, 2048], F32, kind="Internal").ap()
    a2d = nc.dram_tensor("a2d", [TOK, 2048], F32, kind="Internal").ap()
    hsd = nc.dram_tensor("hsd", [TOK, 2048], BF16, kind="Internal").ap()
    shd = nc.dram_tensor("shd", [TOK, D], F32, kind="Internal").ap()
    rld = nc.dram_tensor("rld", [TOK, 16], F32, kind="Internal").ap()
    dwd = nc.dram_tensor("dwd", [TOK, 16], F32, kind="Internal").ap()
    moed = nc.dram_tensor("moed", [TOK, D], F32, kind="Internal").ap()
    pst = _psts(nc)
    with nc.Block() as block:
        ctx = Ctx(nc, block)
        Pb, fb = emit_l3b(ctx, "lb", kvps, qff, yfm, pst)
        SH = ("shx", "shw", 8 * 2048, 2 * 16 * 512)
        p1 = Proj(ctx, "s3", TOK, D, D, [yfm], [wo], s3d, "tok", bias_dram=bo,
                  ps=pst[4:6], x_fm=True, share=SH)
        o1, _ = p1.emit(wait=plan_fin(Pb, fb))
        p2 = Proj(ctx, "a1", TOK, D, 2048, [xn], [sf1], a1d, "tok", ps=pst[6:8], share=SH)
        o2, _ = p2.emit(wait=proj_fin(o1))
        p3 = Proj(ctx, "a2", TOK, D, 2048, [xn], [sf2], a2d, "tok", ps=pst[4:6], share=SH)
        o3, _ = p3.emit(wait=proj_fin(o2))
        Pg, fg = emit_gatemul(ctx, "gm", a1d, a2d, hsd, TOK, 2048,
                              wait=proj_fin(o2) + proj_fin(o3))
        p4 = Proj(ctx, "sh3", TOK, 2048, D, [hsd], [sf3], shd, "tok", ps=pst[6:8], share=SH)
        o4, _ = p4.emit(wait=plan_fin(Pg, fg))
        p5 = Proj(ctx, "rl3", TOK, D, 16, [xn], [gwT], rld, "tok", ps=pst[4:6], share=SH)
        o5, _ = p5.emit(wait=proj_fin(o4))
        Pr, fr_ = emit_route3(ctx, "rt3", rld, dwd, T=TOK, wait=proj_fin(o5))
        Pm, fm_ = emit_moe(ctx, "moe3", xn, w1, None, w2, dwd, moed, 16, 512, False,
                           pst[0], pst[1], pst[2], pst[3], T=TOK, wait=plan_fin(Pr, fr_))
        Pc, fc_ = emit_combine2(ctx, "cbf", xin, s3d, moed, yout, T=TOK, b2_dram=shd,
                                wait=proj_fin(o1) + proj_fin(o4) + plan_fin(Pm, fm_))
        emit_finish_all(ctx, plan_fin(Pc, fc_))
    return nc


def kernel(**ins):
    x = ins["L0_attn_x"]
    dat = ins
    C8 = list(range(NCORE))
    ct = np.ascontiguousarray

    def run(nc, in_maps):
        res = run_bass_kernel_spmd(nc, in_maps, core_ids=C8)
        return res.results

    heads = lambda c: slice(4 * (c % 4) * 64, (4 * (c % 4) + 4) * 64)
    x_slices = [ct(x[c // 4, (c % 4) * 2048:((c % 4) + 1) * 2048]) for c in range(8)]

    r = run(build_ln(), [{"x": s} for s in x_slices])
    xn0 = np.concatenate([r[c]["xn"] for c in range(8)], 0)

    r = run(build_mix0(), [{"xnb": ct(xn0[(c // 4) * S:(c // 4 + 1) * S]),
                            "tmr": dat["L0_attn_time_mix_r"].reshape(1, D),
                            "tmk": dat["L0_attn_time_mix_k"].reshape(1, D),
                            "tmv": dat["L0_attn_time_mix_v"].reshape(1, D)} for c in range(8)])
    mixs = r

    WrT = ct(dat["L0_attn_W_r"].T)
    WkT = ct(dat["L0_attn_W_k"].T)
    WvT = ct(dat["L0_attn_W_v"].T)
    phiT_full = ct(dat["L0_ffn_phi_w"].T)
    ewf = dat["L0_ffn_expert_w"]
    ebf = dat["L0_ffn_expert_b"]
    in_maps = []
    for c in range(8):
        hsl = heads(c)
        esl = slice(c * 8, (c + 1) * 8)
        in_maps.append({
            "xr": mixs[c]["xr"], "xk": mixs[c]["xk"], "xv": mixs[c]["xv"], "xnf": xn0,
            "wr": ct(WrT[:, hsl]), "wk": ct(WkT[:, hsl]), "wv": ct(WvT[:, hsl]),
            "br": ct(dat["L0_attn_b_r"][hsl]), "bk": ct(dat["L0_attn_b_k"][hsl]),
            "bv": ct(dat["L0_attn_b_v"][hsl]),
            "phiT": ct(phiT_full[:, c * 256:(c + 1) * 256]),
            "phib": ct(dat["L0_ffn_phi_b"][c * 256:(c + 1) * 256]),
            "ew": ct(ewf[esl].reshape(8 * 1024, 1024)),
            "eb": ct(ebf[esl].reshape(1, 8192))})
    l0h = run(build_l0head(), in_maps)

    dec = dat["L0_attn_decay"][:, 0, 0]
    eta = dat["L0_attn_eta"][:, 0, 0]
    in_maps = []
    for c in range(8):
        h0 = 4 * (c % 4)
        in_maps.append({"rfm": l0h[c]["rfm"], "krows": l0h[c]["krows"],
                        "vrows": l0h[c]["vrows"],
                        "deceta": ct(np.concatenate([dec[h0:h0 + 4], eta[h0:h0 + 4]]).reshape(1, 8))})
    sc = run(build_scan(), in_maps)

    PH = np.zeros((16, 128), np.float32)
    for hh in range(16):
        PH[hh, (hh % 2) * 64:(hh % 2) * 64 + 64] = 1.0
    CHM = np.zeros((16, 8), np.float32)
    for hh in range(16):
        CHM[hh, hh // 2] = 1.0
    E32 = np.zeros((128, 32), np.float32)
    for p in range(128):
        E32[p, p % 32] = 1.0
    E32T = ct(E32.T)
    wkv_fm_full = np.zeros((2, D, S), np.float32)
    gns_all = np.zeros((2, 16, 2), np.float32)
    for c in range(8):
        b = c // 4
        h0 = 4 * (c % 4)
        wkv_fm_full[b, h0 * 64:(h0 + 4) * 64] = sc[c]["wkvfm"]
        gns_all[b, h0:h0 + 4] = sc[c]["gns"].reshape(4, 2)
    ytil_full = np.concatenate([l0h[c]["ytil"] for c in range(8)], 0)
    woT = ct(dat["L0_attn_W_o"].T)
    wgT = ct(dat["L0_attn_W_g"].T)
    in_maps = []
    for c in range(8):
        b = c // 4
        r0 = (c % 4) * 2048
        prev = xn0[b * S + r0 - 1:b * S + r0] if r0 > 0 else np.zeros((1, D), ml_dtypes.bfloat16)
        in_maps.append({
            "x": x_slices[c], "xn": ct(xn0[b * S + r0:b * S + r0 + 2048]),
            "xnprev": ct(prev), "wkvm": ct(wkv_fm_full[b][:, r0:r0 + 2048]),
            "gnsb": ct(gns_all[b]), "gnw": dat["L0_attn_gn_weight"],
            "gnb": dat["L0_attn_gn_bias"], "PH": PH, "CHM": CHM, "woT": woT,
            "bo": dat["L0_attn_b_o"], "wgT": wgT, "bg": dat["L0_attn_b_g"],
            "tmg": dat["L0_attn_time_mix_g"].reshape(1, D), "phiT": phiT_full,
            "phib": dat["L0_ffn_phi_b"], "ytil": ytil_full, "e32": E32, "e32T": E32T})
    r = run(build_l0tok(), in_maps)
    x1 = np.concatenate([r[c]["x1"] for c in range(8)], 0)
    xn1 = np.concatenate([r[c]["xn1"] for c in range(8)], 0)

    perm128 = np.zeros(128, np.int64)
    perm128[0:32] = np.arange(0, 64, 2)
    perm128[32:64] = 64 + np.arange(0, 64, 2)
    perm128[64:96] = np.arange(1, 64, 2)
    perm128[96:128] = 64 + np.arange(1, 64, 2)
    WqTp = ct(dat["L1_attn_Wq"].reshape(1024, 8, 128)[:, :, perm128].reshape(1024, 1024))
    r = run(build_qzkv(), [{"xn": ct(xn1[c * 2048:(c + 1) * 2048]), "wq": WqTp,
                            "wk": dat["L1_attn_Wk"], "wv": dat["L1_attn_Wv"]} for c in range(8)])
    q1 = np.concatenate([r[c]["q"] for c in range(8)], 0)
    zk1 = np.concatenate([r[c]["zk"] for c in range(8)], 0)
    zv1 = np.concatenate([r[c]["zv"] for c in range(8)], 0)

    kd = dat["L1_attn_key_decoder"]
    vd = dat["L1_attn_value_decoder"]
    half = 32
    freq = 1.0 / 10000.0 ** (np.arange(half, dtype=np.float32) / half)
    fr = np.outer(np.arange(S, dtype=np.float32), freq)
    cosr = ct(np.tile(np.cos(fr).T.astype(np.float32), (2, 1)))
    sinr = ct(np.tile(np.sin(fr).T.astype(np.float32), (2, 1)))

    def mk_Mpos_sign(hlist):
        Mpos = np.zeros((192, 64 * len(hlist)), np.float32)
        Msgn = np.zeros((192, 64 * len(hlist)), np.float32)
        for j, h in enumerate(hlist):
            for f in range(32):
                for rr in range(3):
                    Mpos[f * 6 + rr * 2 + 0, j * 64 + f] = kd[f, h, rr, 0]
                    Msgn[f * 6 + rr * 2 + 0, j * 64 + f] = 1.0
                    Mpos[f * 6 + rr * 2 + 1, j * 64 + f] = kd[f, h, rr, 1]
                    Msgn[f * 6 + rr * 2 + 1, j * 64 + f] = -1.0
                    Mpos[f * 6 + rr * 2 + 0, j * 64 + 32 + f] = kd[f, h, rr, 1]
                    Msgn[f * 6 + rr * 2 + 0, j * 64 + 32 + f] = 1.0
                    Mpos[f * 6 + rr * 2 + 1, j * 64 + 32 + f] = kd[f, h, rr, 0]
                    Msgn[f * 6 + rr * 2 + 1, j * 64 + 32 + f] = 1.0
        return Mpos, Msgn
    in_maps = []
    for c in range(8):
        b = c // 4
        hl = [4 * (c % 4) + i for i in range(4)]
        Mpos, Msgn = mk_Mpos_sign(hl)
        in_maps.append({"qf": ct(q1[b * S:(b + 1) * S, (c % 4) * 256:((c % 4) + 1) * 256]),
                        "zkf": ct(zk1[b * S:(b + 1) * S]), "zvf": ct(zv1[b * S:(b + 1) * S]),
                        "mp": Mpos, "msg": Msgn,
                        "vdc": ct(np.concatenate([vd[h] for h in hl], 1)),
                        "cr": cosr, "sr": sinr})
    r = run(build_l1attn(), in_maps)
    yfm_full = np.zeros((2, D, S), ml_dtypes.bfloat16)
    for c in range(8):
        yfm_full[c // 4, (c % 4) * 256:((c % 4) + 1) * 256] = r[c]["yfm"]

    WgS = ct(dat["L1_ffn_w_gate"].transpose(0, 2, 1).reshape(8 * D, 2048))
    WuS = ct(dat["L1_ffn_w_up"].transpose(0, 2, 1).reshape(8 * D, 2048))
    WdS = ct(dat["L1_ffn_w_down"].transpose(0, 2, 1).reshape(8 * 2048, D))
    in_maps = []
    for c in range(8):
        b = c // 4
        r0 = (c % 4) * 2048
        in_maps.append({"x": ct(x1[c * 2048:(c + 1) * 2048]),
                        "xn": ct(xn1[c * 2048:(c + 1) * 2048]),
                        "yfm": ct(yfm_full[b][:, r0:r0 + 2048]),
                        "wproj": dat["L1_attn_Wproj"],
                        "rwT": ct(dat["L1_ffn_router_w"].T),
                        "wg": WgS, "wu": WuS, "wd": WdS})
    r = run(build_l1tok(), in_maps)
    x2 = np.concatenate([r[c]["x2"] for c in range(8)], 0)
    xn2 = np.concatenate([r[c]["xn2"] for c in range(8)], 0)

    r = run(build_l2pre(), [{"xn": ct(xn2[c * 2048:(c + 1) * 2048]),
                             "wk": dat["L2_attn_Wk"], "wv": dat["L2_attn_Wv"],
                             "wq": dat["L2_attn_Wq"], "wdn": dat["L2_attn_W_down_kv"],
                             "wuk": dat["L2_attn_W_up_k"], "wuv": dat["L2_attn_W_up_v"]}
                            for c in range(8)])
    krec = np.concatenate([r[c]["krec"] for c in range(8)], 0)
    vrec = np.concatenate([r[c]["vrec"] for c in range(8)], 0)
    q2l = [r[c]["q2"] for c in range(8)]

    ascale = np.float32(1.0 / (np.sqrt(np.float32(1024)) + 1e-8))
    aa = 1e-4
    pos = np.arange(S, dtype=np.float64) * aa
    kdp = np.exp(pos).astype(np.float32).reshape(1, S)
    kdm = np.exp(-pos).astype(np.float32).reshape(1, S)
    in_maps = []
    for c in range(8):
        b = c // 4
        r0 = (c % 4) * 2048
        p = np.arange(r0, r0 + 2048, dtype=np.float64) * aa
        in_maps.append({"qfm": q2l[c], "kf": ct(krec[b * S:(b + 1) * S]),
                        "vf": ct(vrec[b * S:(b + 1) * S]),
                        "uqp": (np.exp(p) * ascale).astype(np.float32).reshape(1, 2048),
                        "uqm": (np.exp(-p) * ascale).astype(np.float32).reshape(1, 2048),
                        "kdp": kdp, "kdm": kdm})
    r = run(build_l2attn(), in_maps)
    ofm2 = [r[c]["ofm"] for c in range(8)]

    in_maps = []
    for c in range(8):
        in_maps.append({"x": ct(x2[c * 2048:(c + 1) * 2048]),
                        "xn": ct(xn2[c * 2048:(c + 1) * 2048]), "ofm": ofm2[c],
                        "wo": dat["L2_attn_Wo"],
                        "w1T": ct(dat["L2_ffn_w1_w"].T), "w1s": dat["L2_ffn_w1_s"],
                        "w2T": ct(dat["L2_ffn_w2_w"].T), "w2s": dat["L2_ffn_w2_s"],
                        "h1T": ct(dat["L2_ffn_h1_w"].T), "h1s": dat["L2_ffn_h1_s"],
                        "h2T": ct(dat["L2_ffn_h2_w"].T), "h2s": dat["L2_ffn_h2_s"],
                        "w3T": ct(dat["L2_ffn_w3_w"].T), "w3s": dat["L2_ffn_w3_s"]})
    r = run(build_l2tok(), in_maps)
    x3 = np.concatenate([r[c]["x3"] for c in range(8)], 0)
    xn3 = np.concatenate([r[c]["xn3"] for c in range(8)], 0)

    in_maps = [{"xn": ct(xn3[c * 2048:(c + 1) * 2048]), "wq": dat["L3_attn_Wq"],
                "wk": dat["L3_attn_Wk"], "wv": dat["L3_attn_Wv"],
                "wf": dat["L3_attn_Wf"], "bf": dat["L3_attn_bf"]} for c in range(8)]
    r = run(build_l3a(), in_maps)
    kvp = [r[c]["kvp"] for c in range(8)]
    qffl = [r[c]["qff"] for c in range(8)]
    in_maps = []
    for c in range(8):
        g0 = (c // 4) * 4
        stack = np.concatenate([kvp[g0 + j] for j in range(4)] + [np.zeros_like(kvp[0])] * 4, 0)
        in_maps.append({"x": ct(x3[c * 2048:(c + 1) * 2048]),
                        "xn": ct(xn3[c * 2048:(c + 1) * 2048]),
                        "kvps": ct(stack), "qff": qffl[c],
                        "wo": dat["L3_attn_Wo"], "bo": dat["L3_attn_bo"],
                        "gwT": ct(dat["L3_ffn_gate_w"].T),
                        "w1": ct(dat["L3_ffn_w1"].transpose(0, 2, 1).reshape(16 * D, 512)),
                        "w2": ct(dat["L3_ffn_w2"].transpose(0, 2, 1).reshape(16 * 512, D)),
                        "sf1": ct(dat["L3_ffn_sfc1"].T), "sf2": ct(dat["L3_ffn_sfc2"].T),
                        "sf3": ct(dat["L3_ffn_sfc3"].T)})
    r = run(build_l3b(), in_maps)
    y = np.concatenate([r[c]["y"] for c in range(8)], 0)
    return y.reshape(2, S, D)
